# version 07
# baseline (speedup 1.0000x reference)
import contextlib
import numpy as np
import ml_dtypes
import concourse.bacc as bacc
import concourse.bass as bass
import concourse.mybir as mybir
from concourse.bass_utils import run_bass_kernel_spmd

F32 = mybir.dt.float32
BF16 = mybir.dt.bfloat16
ALU = mybir.AluOpType
ACTF = mybir.ActivationFunctionType
AX = mybir.AxisListType

B, T, D, H = 2, 8192, 2048, 16
SEG, C = 2048, 128
NCH = SEG // C
DVP = 256
LEVELS = (8, 16, 32, 64)
NT4 = SEG // 512
NDT = D // 128
NIT = 8192 // 128
NSL = ((0, 512), (512, 1024), (1024, 1536), (1536, 2048), (2048, 2051))

ENGINES = ("sync", "scalar", "tensor", "vector", "gpsimd")
DMA_SLOTS = {"sync": 10, "scalar": 6, "tensor": 1, "vector": 1, "gpsimd": 8}

CFVALS = (1e-5, 128.0 * 1e-6, 1e-6, 128.0, -1.0, 1.0 / 128.0)


class _Op:
    __slots__ = ("eng", "fn", "reads", "writes", "dma", "val_after",
                 "waits", "slot", "slot_use")

    def __init__(self, eng, fn, reads, writes, dma):
        self.eng, self.fn = eng, fn
        self.reads, self.writes, self.dma = reads, writes, dma
        self.val_after = 0
        self.waits = []
        self.slot = None
        self.slot_use = 0


class Prog:

    def __init__(self):
        self.ops = {e: [] for e in ENGINES}
        self.last_write = {}
        self.readers = {}
        self.needed = set()
        self.barrier_ops = []
        self.dmas_since_barrier = []
        self.n_ops = 0

    def emit(self, eng, fn, reads=(), writes=(), dma=False):
        op = _Op(eng, fn, tuple(reads), tuple(writes), dma)
        self.ops[eng].append(op)
        self.n_ops += 1
        if dma:
            self.dmas_since_barrier.append(op)
        deps = list(self.barrier_ops)
        for b_ in op.reads:
            w = self.last_write.get(b_)
            if w is not None:
                deps.append(w)
        for b_ in op.writes:
            w = self.last_write.get(b_)
            if w is not None:
                deps.append(w)
            deps.extend(self.readers.get(b_, ()))
        seen = set()
        for d in deps:
            if d is op or id(d) in seen:
                continue
            seen.add(id(d))
            op.waits.append(d)
            self.needed.add(d)
        for b_ in op.reads:
            self.readers.setdefault(b_, []).append(op)
        for b_ in op.writes:
            self.last_write[b_] = op
            self.readers[b_] = []
        return op

    def barrier(self):
        last = [self.ops[e][-1] for e in ENGINES if self.ops[e]]
        self.barrier_ops = last + self.dmas_since_barrier
        for d in last:
            self.needed.add(d)
        self.dmas_since_barrier = []
        self.last_write = {}
        self.readers = {}

    def finalize(self, nc, ctx):
        slot_uses = {}
        for e in ENGINES:
            rot = 0
            for op in self.ops[e]:
                if op.dma:
                    s = (e, rot % DMA_SLOTS[e])
                    rot += 1
                    op.slot = s
                    op.slot_use = slot_uses.get(s, 0)
                    slot_uses[s] = op.slot_use + 1
        for e in ENGINES:
            for op in reversed(self.ops[e]):
                if not op.dma:
                    self.needed.add(op)
                    break
        for e in ENGINES:
            v = 0
            for op in self.ops[e]:
                if not op.dma and op in self.needed:
                    v += 1
                op.val_after = v
        sems = {e: ctx.enter_context(nc.semaphore(f"sem_{e}"))
                for e in ENGINES if any(not o.dma for o in self.ops[e])}
        slot_sems = {s: ctx.enter_context(nc.semaphore(f"dma_{s[0]}_{s[1]}"))
                     for s in slot_uses}
        block = ctx.enter_context(nc.Block())

        def run_engine(e):
            def body(engine):
                cur = {e2: 0 for e2 in ENGINES}
                cur_slot = {s: 0 for s in slot_sems}
                for op in self.ops[e]:
                    for d in op.waits:
                        if d.dma:
                            tgt = 16 * (d.slot_use + 1)
                            if cur_slot[d.slot] < tgt:
                                engine.wait_ge(slot_sems[d.slot], tgt)
                                cur_slot[d.slot] = tgt
                        elif cur[d.eng] < d.val_after:
                            engine.wait_ge(sems[d.eng], d.val_after)
                            cur[d.eng] = d.val_after
                    if op.dma:
                        tgt = 16 * op.slot_use
                        if tgt > 0 and cur_slot[op.slot] < tgt:
                            engine.wait_ge(slot_sems[op.slot], tgt)
                            cur_slot[op.slot] = tgt
                        op.fn(engine).then_inc(slot_sems[op.slot], 16)
                    else:
                        inst = op.fn(engine)
                        if op in self.needed:
                            if inst is None:
                                inst = engine.nop()
                            inst.then_inc(sems[e], 1)
            return body

        for e in ENGINES:
            if self.ops[e]:
                getattr(block, e)(run_engine(e))


def host_consts():
    bf = ml_dtypes.bfloat16
    ident = np.eye(128, dtype=bf)
    bandmask = np.zeros((128, 18), dtype=bf)
    for s in range(128):
        for d in range(1, 8):
            if (s % 8) + d <= 7:
                bandmask[s, d] = 1.0
                bandmask[s, 8 + d] = 1.0
        bandmask[s, 8] = 1.0
        bandmask[s, 16] = 1.0
        bandmask[s, 17] = 1.0
    ohbig = np.zeros((128, 18 * 18), dtype=bf)
    for j in range(1, 8):
        ohbig[:, j * 18 + j] = -1.0
    for d in range(0, 8):
        ohbig[:, (8 + d) * 18 + 8 + d] = 1.0
    ohbig[0, 16 * 18 + 16] = -1.0
    shA = np.zeros((128, 128), dtype=bf)
    shB = np.zeros((128, 128), dtype=bf)
    for j in range(4):
        for i in range(8):
            m = 32 * j + i
            (shA if j % 2 == 0 else shB)[m, m + 16] = 1.0
    onescol = np.ones((128, 2), dtype=bf)
    onesrow = np.ones((1, 128), dtype=bf)
    c128row = np.full((1, 128), -128.0, dtype=bf)
    hsel = np.zeros((16, 16 * 128), dtype=bf)
    for h in range(16):
        hsel[h, h * 128:(h + 1) * 128] = 1.0
    cflP = np.tile(np.array([CFVALS], dtype=np.float32), (128, 1))
    identf = np.eye(16, dtype=np.float32)
    return dict(ident=ident, bandmask=bandmask, ohbig=ohbig, shA=shA,
                shB=shB, onescol=onescol, onesrow=onesrow, c128row=c128row,
                hsel=hsel, cflP=cflP, identf=identf)


def build_core_program():
    nc = bacc.Bacc("TRN2", debug=False)
    dr = {}

    def din(name, shape, dt):
        dr[name] = nc.dram_tensor(name, shape, dt, kind="ExternalInput").ap()

    din("xT", [D, SEG], F32)
    din("xhT", [D, 3], F32)
    din("gam", [128, 8], F32)
    din("WqP", [H, D, 128], F32)
    din("WkP", [H, D, 128], F32)
    din("WvP", [H, D, 128], F32)
    din("WgP", [H, D, 128], F32)
    din("Wf1", [D, 128], F32)
    din("Wf2P", [H, 128, 128], F32)
    din("WbP", [D, 16], F32)
    din("WoP", [H, 128, D], F32)
    din("convP", [H, 128, 12], F32)
    din("dtbP", [128, H], F32)
    din("AlogP", [16, 1], F32)
    din("onwP", [1, 128], F32)
    din("bgP", [H, 128], F32)
    din("gwP", [NIT, D, 128], BF16)
    din("uwP", [NIT, D, 128], BF16)
    din("dwP", [16, NIT, 128, 128], BF16)
    din("gsP", [NIT, 128], F32)
    din("usP", [NIT, 128], F32)
    din("dsP", [128, 16], F32)
    for nm, shp, dt in (("ident", [128, 128], BF16),
                        ("bandmask", [128, 18], BF16),
                        ("ohbig", [128, 324], BF16),
                        ("shA", [128, 128], BF16), ("shB", [128, 128], BF16),
                        ("onescol", [128, 2], BF16),
                        ("onesrow", [1, 128], BF16),
                        ("c128row", [1, 128], BF16),
                        ("hsel", [16, 2048], BF16),
                        ("cflP", [128, len(CFVALS)], F32),
                        ("identf", [16, 16], F32)):
        din(nm, shp, dt)
    outT = nc.dram_tensor("outT", [D, SEG], F32, kind="ExternalOutput").ap()
    shear = nc.dram_tensor("shearbuf", [2, NCH, 16512], BF16).ap()
    fq = nc.dram_tensor("fqbuf", [H, 128, SEG], BF16).ap()
    fk = nc.dram_tensor("fkbuf", [H, 128, SEG], BF16).ap()
    fv = nc.dram_tensor("fvbuf", [H, 128, SEG], BF16).ap()
    fg = nc.dram_tensor("fgbuf", [H, 128, SEG], F32).ap()
    fbb = nc.dram_tensor("fbbuf", [H, 128, SEG], BF16).ap()
    obuf = nc.dram_tensor("obuf", [H, 128, SEG], BF16).ap()
    rbuf = nc.dram_tensor("respbuf", [H, 128, SEG], BF16).ap()
    sgate = nc.dram_tensor("sgatebuf", [H, 128, SEG], BF16).ap()
    x2buf = nc.dram_tensor("x2buf", [D, SEG], F32).ap()
    hhbuf = nc.dram_tensor("hhbuf", [NIT, 128, SEG], BF16).ap()
    shbuf = nc.dram_tensor("shbuf", [2, SEG], BF16).ap()
    agin = nc.dram_tensor("aginbuf", [H * 128, DVP], BF16).ap()
    agout = nc.dram_tensor("agoutbuf", [8 * H * 128, DVP], BF16).ap()

    pr = Prog()
    E = pr.emit
    gctx = contextlib.ExitStack()
    with gctx:
        def gsb(name, shape, dt=F32):
            return gctx.enter_context(nc.sbuf_tensor(name, shape, dt))

        ident = gsb("c_ident", [128, 128], BF16)
        bmask = gsb("c_bandmask", [128, 18], BF16)
        ohbig = gsb("c_ohbig", [128, 324], BF16)
        shmA = gsb("c_shA", [128, 128], BF16)
        shmB = gsb("c_shB", [128, 128], BF16)
        onescol = gsb("c_onescol", [128, 2], BF16)
        onesrow = gsb("c_onesrow", [1, 128], BF16)
        onesr5 = gsb("c_onesr5", [1, 512], BF16)
        c128row = gsb("c_c128row", [1, 128], BF16)
        hsel = gsb("c_hsel", [16, 2048], BF16)
        gam = gsb("c_gam", [128, 8], F32)
        onw = gsb("c_onw", [128, 128], BF16)
        aposP = gsb("c_aposP", [128, 16], F32)
        dtb = gsb("c_dtb", [128, H], F32)
        dtbn = gsb("c_dtbn", [128, H], F32)
        scr16 = gsb("c_scr16", [16, 8], F32)
        cfl = gsb("c_cfl", [128, len(CFVALS)], F32)
        for ci, cv in enumerate(CFVALS):
            nc.const_aps.register(F32, cv, cfl[0:128, ci:ci + 1])
        identf = gsb("c_identf", [16, 16], F32)
        aprow = gsb("c_aprow", [1, 16], F32)

        psb = {i: gctx.enter_context(
            nc.psum_tensor(f"ps_b{i}", [128, 512], F32)) for i in range(7)}
        pstp = gctx.enter_context(nc.psum_tensor("ps_tp", [128, 1024], BF16))

        def MM(out_ap, lhsT_ap, rhs_ap, reads, bank, start, stop, tp=None):
            E("tensor", (lambda o, l, r, s1, s2, tpp: lambda t: t.matmul(
                o, l, r, start=s1, stop=s2, tile_position=tpp))(
                out_ap, lhsT_ap, rhs_ap, start, stop, tp),
              reads=tuple(reads) + (bank,), writes=(bank,))

        def TPo(out_ap, in_ap, id_ap, reads, bank):
            E("tensor", (lambda o, i, d: lambda t: t.transpose(o, i, d))(
                out_ap, in_ap, id_ap),
              reads=tuple(reads) + ("c_ident", bank), writes=(bank,))

        def VCP(out_ap, in_ap, reads, writes):
            E("vector", (lambda o, i: lambda v: v.tensor_copy(o, i))(
                out_ap, in_ap), reads=reads, writes=writes)

        def ACP(out_ap, in_ap, reads, writes):
            E("scalar", (lambda o, i: lambda s: s.copy(o, i))(out_ap, in_ap),
              reads=reads, writes=writes)

        def TT(out_ap, a_ap, b_ap, op, reads, writes):
            E("vector", (lambda o, a, b_, p: lambda v: v.tensor_tensor(
                o, a, b_, p))(out_ap, a_ap, b_ap, op),
              reads=reads, writes=writes)

        def TSm(out_ap, a_ap, scalar, reads, writes):
            E("vector", (lambda o, a, c: lambda v: v.tensor_scalar_mul(
                o, a, c))(out_ap, a_ap, scalar), reads=reads, writes=writes)

        def STT(out_ap, in0_ap, scalar, in1_ap, op0, op1, reads, writes):
            E("vector", (lambda o, a, s, b_, p0, p1: lambda v:
                         v.scalar_tensor_tensor(o, a, s, b_, p0, p1))(
                out_ap, in0_ap, scalar, in1_ap, op0, op1),
              reads=reads, writes=writes)

        def ACT(out_ap, in_ap, func, reads, writes, bias=0.0, scale=1.0):
            extra = ()
            if (isinstance(bias, float) and bias != 0.0) or \
                    (isinstance(scale, float) and scale != 1.0):
                extra = ("c_cfl",)
            E("scalar", (lambda o, i, f, bb, sc: lambda s: s.activation(
                o, i, f, bias=bb, scale=sc))(out_ap, in_ap, func, bias,
                                             scale),
              reads=tuple(reads) + extra, writes=writes)

        def DMA(eng, out_ap, in_ap, reads, writes):
            E(eng, (lambda o, i: lambda s: s.dma_start(out=o, in_=i))(
                out_ap, in_ap), reads=reads, writes=writes, dma=True)

        def MS(ap, val, writes, reads=()):
            E("vector", (lambda a, vv: lambda v: v.memset(a, vv))(ap, val),
              reads=reads, writes=writes)

        def RCP(ap, reads, writes):
            E("vector", (lambda a: lambda v: v.reciprocal(a, a))(ap),
              reads=reads, writes=writes)

        def dsl(t_ap, i, p_, n_):
            return bass.AP(t_ap.tensor, i * p_ * n_, [[n_, p_], [1, n_]])

        for nm, dst in (("ident", ident), ("bandmask", bmask),
                        ("ohbig", ohbig), ("shA", shmA), ("shB", shmB),
                        ("onescol", onescol), ("onesrow", onesrow),
                        ("c128row", c128row), ("hsel", hsel), ("gam", gam)):
            DMA("sync", dst[:], dr[nm], (), (f"c_{nm}",))
        MS(onesr5[:], 1.0, ("c_onesr5",))
        DMA("sync", scr16[0:16, 0:1], dr["AlogP"], (), ("alog",))
        DMA("sync", cfl[:], dr["cflP"], (), ("c_cfl",))
        DMA("sync", dtb[:], dr["dtbP"], (), ("c_dtb",))
        TSm(dtbn[:], dtb[:], -1.0, ("c_dtb",), ("c_dtbn",))
        ACT(scr16[0:16, 1:2], scr16[0:16, 0:1], ACTF.Exp, ("alog",),
            ("alog",))
        DMA("sync", identf[:], dr["identf"], (), ("c_identf",))
        E("tensor", lambda t: t.transpose(
            psb[6][0:1, 16:32], scr16[0:16, 1:2], identf[0:16, 0:16]),
          reads=("alog", "c_identf", "psb6"), writes=("psb6",))
        VCP(aprow[0:1, 0:16], psb[6][0:1, 16:32], ("psb6",), ("c_aprow",))
        E("gpsimd", lambda gp: gp.partition_broadcast(
            aposP[:], aprow[0:1, 0:16]),
          reads=("c_aprow",), writes=("c_aposP",))
        DMA("gpsimd", onw[0:1, :], dr["onwP"], ("c_onw",), ("onwrow",))
        MM(psb[6][0:128, 128:256], onesrow[:], onw[0:1, :],
           ("onwrow", "c_onesrow"), "psb6", start=True, stop=True)
        VCP(onw[:], psb[6][0:128, 128:256], ("psb6",), ("c_onw",))

        def emit_layernorm(tag, t, src_ap, src_key, xt, xkey, sq, sqkey):
            ks = {k_: f"{tag}_{k_}" for k_ in t}
            for dt_ in range(NDT):
                DMA("sync", t["xs"][:], src_ap[dt_ * 128:(dt_ + 1) * 128, :],
                    (src_key, ks["xs"]), (ks["xs"],))
                ACP(xt[dt_][:, 3:SEG + 3], t["xs"][:], (ks["xs"],),
                    (xkey(dt_),))
                ACT(sq[:, 0:SEG], t["xs"][:], ACTF.Square, (ks["xs"],),
                    (sqkey,))
                for n in range(NT4):
                    MM(psb[n][0:1, 0:512], onescol[:, 0:1],
                       xt[dt_][:, 3 + n * 512:3 + (n + 1) * 512],
                       (xkey(dt_), "c_onescol"), f"psb{n}",
                       start=(dt_ == 0), stop=(dt_ == NDT - 1))
                    MM(psb[n][32:33, 0:512], onescol[:, 0:1],
                       sq[:, n * 512:(n + 1) * 512],
                       (sqkey, "c_onescol"), f"psb{n}",
                       start=(dt_ == 0), stop=(dt_ == NDT - 1))
            for n in range(NT4):
                VCP(t["smrow"][0:1, n * 512:(n + 1) * 512],
                    psb[n][0:1, 0:512], (f"psb{n}",), (ks["smrow"],))
                VCP(t["sqrow"][0:1, n * 512:(n + 1) * 512],
                    psb[n][32:33, 0:512], (f"psb{n}",), (ks["sqrow"],))
            TSm(t["smrow"][:], t["smrow"][:], 1.0 / D, (ks["smrow"],),
                (ks["smrow"],))
            ACP(t["mrb"][:], t["smrow"][:], (ks["smrow"],), (ks["mrb"],))
            TSm(t["sqrow"][:], t["sqrow"][:], 1.0 / D, (ks["sqrow"],),
                (ks["sqrow"],))
            TT(t["smrow"][:], t["smrow"][:], t["smrow"][:], ALU.mult,
               (ks["smrow"],), (ks["smrow"],))
            TT(t["sqrow"][:], t["sqrow"][:], t["smrow"][:], ALU.subtract,
               (ks["sqrow"], ks["smrow"]), (ks["sqrow"],))
            ACT(t["sqrow"][:], t["sqrow"][:], ACTF.Sqrt,
                (ks["sqrow"],), (ks["sqrow"],), bias=1e-5)
            RCP(t["sqrow"][:], (ks["sqrow"],), (ks["sqrow"],))
            ACP(t["ivb"][:], t["sqrow"][:], (ks["sqrow"],), (ks["ivb"],))
            for n in range(NT4):
                MM(psb[4][0:128, 0:512], onesrow[:],
                   t["mrb"][0:1, n * 512:(n + 1) * 512],
                   (ks["mrb"], "c_onesrow"), "psb4", start=True, stop=True)
                VCP(t["murepb"][:, n * 512:(n + 1) * 512],
                    psb[4][0:128, 0:512], ("psb4",), (ks["murepb"],))
                MM(psb[5][0:128, 0:512], onesrow[:],
                   t["ivb"][0:1, n * 512:(n + 1) * 512],
                   (ks["ivb"], "c_onesrow"), "psb5", start=True, stop=True)
                VCP(t["ivrepb"][:, n * 512:(n + 1) * 512],
                    psb[5][0:128, 0:512], ("psb5",), (ks["ivrepb"],))
            for dt_ in range(NDT):
                TT(xt[dt_][:, 3:SEG + 3], xt[dt_][:, 3:SEG + 3],
                   t["murepb"][:], ALU.subtract,
                   (xkey(dt_), ks["murepb"]), (xkey(dt_),))
                TT(xt[dt_][:, 3:SEG + 3], xt[dt_][:, 3:SEG + 3],
                   t["ivrepb"][:], ALU.mult,
                   (xkey(dt_), ks["ivrepb"]), (xkey(dt_),))

        ar1 = bass.AutoArena(nc)
        xln = {i: ar1.sbuf_tensor(f"p_xln{i}", [128, SEG + 3], BF16)
               for i in range(NDT)}
        xh = {i: ar1.sbuf_tensor(f"p_xh{i}", [128, 3], F32)
              for i in range(NDT)}
        xh3sq = ar1.sbuf_tensor("p_xh3sq", [128, 4], BF16)
        hrow = ar1.sbuf_tensor("p_hrow", [1, 16], F32)
        hrowb = ar1.sbuf_tensor("p_hrowb", [1, 16], BF16)
        ln1 = {"xs": ar1.sbuf_tensor("p_xs", [128, SEG], F32),
               "smrow": ar1.sbuf_tensor("p_smrow", [1, SEG], F32),
               "sqrow": ar1.sbuf_tensor("p_sqrow", [1, SEG], F32),
               "mrb": ar1.sbuf_tensor("p_mrb", [1, SEG], BF16),
               "ivb": ar1.sbuf_tensor("p_ivb", [1, SEG], BF16),
               "murepb": ar1.sbuf_tensor("p_murepb", [128, SEG], BF16),
               "ivrepb": ar1.sbuf_tensor("p_ivrepb", [128, SEG], BF16)}
        wbuf = {i: ar1.sbuf_tensor(f"p_wbuf{i}", [128, 2048], BF16)
                for i in range(2)}
        wsmall = ar1.sbuf_tensor("p_wsmall", [128, 256], BF16)
        zq = ar1.sbuf_tensor("p_zq", [128, SEG + 3], BF16)
        zk = ar1.sbuf_tensor("p_zk", [128, SEG + 3], BF16)
        zv = ar1.sbuf_tensor("p_zv", [128, SEG + 3], BF16)
        cq = ar1.sbuf_tensor("p_cq", [128, SEG], BF16)
        ck = ar1.sbuf_tensor("p_ck", [128, SEG], BF16)
        cvv = ar1.sbuf_tensor("p_cvv", [128, SEG], BF16)
        sqf = ar1.sbuf_tensor("p_sqf", [128, 512], BF16)
        ssrow = ar1.sbuf_tensor("p_ssrow", [1, SEG], F32)
        ssb = ar1.sbuf_tensor("p_ssb", [1, SEG], BF16)
        ssrep = ar1.sbuf_tensor("p_ssrep", [128, 512], BF16)
        zf1 = ar1.sbuf_tensor("p_zf1", [128, SEG], BF16)
        bsig = ar1.sbuf_tensor("p_bsig", [16, SEG], BF16)
        brep = ar1.sbuf_tensor("p_brep", [128, SEG], BF16)
        gsg = ar1.sbuf_tensor("p_gsg", [128, 512], F32)
        gfl = ar1.sbuf_tensor("p_gfl", [128, 512], F32)
        convh = ar1.sbuf_tensor("p_convh", [128, 12], F32)
        wf2h = ar1.sbuf_tensor("p_wf2h", [128, 128], BF16)
        bgh = ar1.sbuf_tensor("p_bgh", [1, 128], BF16)
        gts = ar1.sbuf_tensor("p_gts", [128, SEG], BF16)
        ar1.allocate()

        emit_layernorm("p", ln1, dr["xT"], "xT", xln, lambda d: f"xln{d}",
                       zq, "zq")
        for dt_ in range(NDT):
            DMA("sync", xh[dt_][:], dr["xhT"][dt_ * 128:(dt_ + 1) * 128, :],
                (), (f"xh{dt_}",))
        for dt_ in range(NDT):
            ACP(xln[dt_][:, 0:3], xh[dt_][:], (f"xh{dt_}",), (f"xln{dt_}",))
            ACT(xh3sq[:, 0:3], xh[dt_][:], ACTF.Square, (f"xh{dt_}",),
                ("xh3sq",))
            MM(psb[6][0:1, 0:3], onescol[:, 0:1], xln[dt_][:, 0:3],
               (f"xln{dt_}", "c_onescol"), "psb6",
               start=(dt_ == 0), stop=(dt_ == NDT - 1))
            MM(psb[6][32:33, 0:3], onescol[:, 0:1], xh3sq[:, 0:3],
               ("xh3sq", "c_onescol"), "psb6",
               start=(dt_ == 0), stop=(dt_ == NDT - 1))
        VCP(hrow[0:1, 0:3], psb[6][0:1, 0:3], ("psb6",), ("hrow",))
        VCP(hrow[0:1, 8:11], psb[6][32:33, 0:3], ("psb6",), ("hrow",))
        TSm(hrow[0:1, 0:3], hrow[0:1, 0:3], 1.0 / D, ("hrow",), ("hrow",))
        ACP(hrowb[0:1, 0:3], hrow[0:1, 0:3], ("hrow",), ("hrowb",))
        TSm(hrow[0:1, 8:11], hrow[0:1, 8:11], 1.0 / D, ("hrow",), ("hrow",))
        TT(hrow[0:1, 4:7], hrow[0:1, 0:3], hrow[0:1, 0:3], ALU.mult,
           ("hrow",), ("hrow",))
        TT(hrow[0:1, 8:11], hrow[0:1, 8:11], hrow[0:1, 4:7], ALU.subtract,
           ("hrow",), ("hrow",))
        ACT(hrow[0:1, 8:11], hrow[0:1, 8:11], ACTF.Sqrt, ("hrow",),
            ("hrow",), bias=1e-5)
        RCP(hrow[0:1, 8:11], ("hrow",), ("hrow",))
        ACP(hrowb[0:1, 8:11], hrow[0:1, 8:11], ("hrow",), ("hrowb",))
        MM(psb[6][0:128, 0:3], onesrow[:], hrowb[0:1, 0:3],
           ("hrowb", "c_onesrow"), "psb6", start=True, stop=True)
        VCP(ln1["murepb"][:, 0:3], psb[6][0:128, 0:3], ("psb6",),
            ("p_murepb",))
        MM(psb[6][0:128, 4:7], onesrow[:], hrowb[0:1, 8:11],
           ("hrowb", "c_onesrow"), "psb6", start=True, stop=True)
        VCP(ln1["ivrepb"][:, 0:3], psb[6][0:128, 4:7], ("psb6",),
            ("p_ivrepb",))
        for dt_ in range(NDT):
            TT(xln[dt_][:, 0:3], xln[dt_][:, 0:3], ln1["murepb"][:, 0:3],
               ALU.subtract, (f"xln{dt_}", "p_murepb"), (f"xln{dt_}",))
            TT(xln[dt_][:, 0:3], xln[dt_][:, 0:3], ln1["ivrepb"][:, 0:3],
               ALU.mult, (f"xln{dt_}", "p_ivrepb"), (f"xln{dt_}",))

        DMA("gpsimd", wsmall[:],
            bass.AP(dr["WbP"].tensor, 0, [[16, 128], [2048, 16], [1, 16]]),
            (), ("wb",))
        for n in range(NT4):
            for dt_ in range(NDT):
                MM(psb[n][0:16, 0:512], wsmall[:, dt_ * 16:dt_ * 16 + 16],
                   xln[dt_][:, 3 + n * 512:3 + (n + 1) * 512],
                   (f"xln{dt_}", "wb"), f"psb{n}",
                   start=(dt_ == 0), stop=(dt_ == NDT - 1))
        for n in range(NT4):
            ACT(bsig[0:16, n * 512:(n + 1) * 512], psb[n][0:16, 0:512],
                ACTF.Sigmoid, (f"psb{n}",), ("bsig",))
        DMA("gpsimd", wbuf[1][:],
            bass.AP(dr["Wf1"].tensor, 0,
                    [[128, 128], [128 * 128, 16], [1, 128]]),
            ("pw1",), ("pw1",))
        for n in range(NT4):
            for dt_ in range(NDT):
                MM(psb[4][0:128, 0:512],
                   wbuf[1][:, dt_ * 128:(dt_ + 1) * 128],
                   xln[dt_][:, 3 + n * 512:3 + (n + 1) * 512],
                   (f"xln{dt_}", "pw1"), "psb4",
                   start=(dt_ == 0), stop=(dt_ == NDT - 1))
            ACP(zf1[:, n * 512:(n + 1) * 512], psb[4][0:128, 0:512],
                ("psb4",), ("zf1",))

        for h in range(H):
            ws = wbuf[h % 2]
            wk_ = f"pw{h % 2}"
            for mi, (wnm, ztile, zkey) in enumerate(
                    (("WqP", zq, "zq"), ("WkP", zk, "zk"),
                     ("WvP", zv, "zv"))):
                DMA("gpsimd", ws[:],
                    bass.AP(dr[wnm].tensor, h * D * 128,
                            [[128, 128], [128 * 128, 16], [1, 128]]),
                    (wk_,), (wk_,))
                for (a, b_) in NSL:
                    for dt_ in range(NDT):
                        MM(psb[mi][0:128, 0:b_ - a],
                           ws[:, dt_ * 128:(dt_ + 1) * 128],
                           xln[dt_][:, a:b_],
                           (f"xln{dt_}", wk_), f"psb{mi}",
                           start=(dt_ == 0), stop=(dt_ == NDT - 1))
                    ACP(ztile[:, a:b_], psb[mi][0:128, 0:b_ - a],
                        (f"psb{mi}",), (zkey,))
            DMA("sync", convh[:], dsl(dr["convP"], h, 128, 12),
                ("convh",), ("convh",))
            for (ztile, zkey, co, cokey, c0) in (
                    (zq, "zq", cq, "cq", 0), (zk, "zk", ck, "ck", 4),
                    (zv, "zv", cvv, "cvv", 8)):
                MS(co[:], 0.0, (cokey,))
                for j in (2, 3, 1, 0):
                    STT(co[:], ztile[:, j:SEG + j],
                        convh[:, c0 + j:c0 + j + 1], co[:],
                        ALU.mult, ALU.add,
                        ("convh", zkey, cokey), (cokey,))
            ACT(cq[:], cq[:], ACTF.Silu, ("cq",), ("cq",))
            ACT(ck[:], ck[:], ACTF.Silu, ("ck",), ("ck",))
            ACT(cvv[:], cvv[:], ACTF.Silu, ("cvv",), ("cvv",))
            for n in range(NT4):
                MM(psb[3][0:128, 0:512], hsel[:, h * 128:(h + 1) * 128],
                   bsig[0:16, n * 512:(n + 1) * 512],
                   ("bsig", "c_hsel"), "psb3", start=True, stop=True)
                ACP(brep[:, n * 512:(n + 1) * 512], psb[3][0:128, 0:512],
                    ("psb3",), ("brep",))
            TT(cvv[:], cvv[:], brep[:], ALU.mult, ("cvv", "brep"), ("cvv",))
            DMA("scalar", dsl(fv, h, 128, SEG), cvv[:], ("cvv",), ("w_fv",))
            DMA("scalar", dsl(fbb, h, 128, SEG), brep[:], ("brep",),
                ("w_fbb",))
            for (src, skey, dstb, fold) in ((cq, "cq", fq, True),
                                            (ck, "ck", fk, False)):
                for n in range(NT4):
                    ACT(sqf[:], src[:, n * 512:(n + 1) * 512], ACTF.Square,
                        (skey,), ("sqf",))
                    MM(psb[4][0:1, 0:512], onescol[:, 0:1], sqf[:],
                       ("sqf", "c_onescol"), "psb4", start=True, stop=True)
                    VCP(ssrow[0:1, n * 512:(n + 1) * 512],
                        psb[4][0:1, 0:512], ("psb4",), ("ssrow",))
                ACT(ssrow[:], ssrow[:], ACTF.Sqrt, ("ssrow",), ("ssrow",),
                    bias=(128.0 * 1e-6 if fold else 1e-6),
                    scale=(128.0 if fold else 1.0))
                RCP(ssrow[:], ("ssrow",), ("ssrow",))
                ACP(ssb[:], ssrow[:], ("ssrow",), ("ssb",))
                for n in range(NT4):
                    MM(psb[4][0:128, 0:512], onesrow[:],
                       ssb[0:1, n * 512:(n + 1) * 512],
                       ("ssb", "c_onesrow"), "psb4", start=True, stop=True)
                    VCP(ssrep[:], psb[4][0:128, 0:512], ("psb4",),
                        ("ssrep",))
                    TT(src[:, n * 512:(n + 1) * 512],
                       src[:, n * 512:(n + 1) * 512], ssrep[:], ALU.mult,
                       (skey, "ssrep"), (skey,))
                DMA("scalar", dsl(dstb, h, 128, SEG), src[:], (skey,),
                    ("w_fql",))
            DMA("gpsimd", wf2h[:], dsl(dr["Wf2P"], h, 128, 128),
                ("wf2h",), ("wf2h",))
            for n in range(NT4):
                MM(psb[5][0:128, 0:512], wf2h[:],
                   zf1[:, n * 512:(n + 1) * 512],
                   ("zf1", "wf2h"), "psb5", start=True, stop=True)
                E("scalar", (lambda h_: lambda s: s.activation(
                    gsg[:], psb[5][0:128, 0:512], ACTF.Sigmoid,
                    bias=dtbn[:, h_:h_ + 1], scale=-1.0))(h),
                  reads=("psb5", "c_dtbn", "c_cfl"), writes=("gsg",))
                ACT(gsg[:], gsg[:], ACTF.Ln, ("gsg",), ("gsg",))
                E("scalar", (lambda h_: lambda s: s.activation(
                    gfl[:], gsg[:], ACTF.Exp,
                    scale=aposP[:, h_:h_ + 1]))(h),
                  reads=("gsg", "c_aposP"), writes=("gfl",))
                DMA("scalar",
                    bass.AP(fg.tensor, h * 128 * SEG + n * 512,
                            [[SEG, 128], [1, 512]]),
                    gfl[:], ("gfl",), ("w_fg",))
            DMA("gpsimd", ws[:],
                bass.AP(dr["WgP"].tensor, h * D * 128,
                        [[128, 128], [128 * 128, 16], [1, 128]]),
                (wk_,), (wk_,))
            DMA("gpsimd", bgh[:],
                bass.AP(dr["bgP"].tensor, h * 128, [[128, 1], [1, 128]]),
                ("bgh",), ("bgh",))
            for ch in range(NCH):
                for dt_ in range(NDT):
                    MM(psb[6][0:128, 0:128],
                       xln[dt_][:, 3 + ch * 128:3 + (ch + 1) * 128],
                       ws[:, dt_ * 128:(dt_ + 1) * 128],
                       (f"xln{dt_}", wk_), "psb6",
                       start=(dt_ == 0), stop=False)
                MM(psb[6][0:128, 0:128], onesrow[:], bgh[:],
                   ("bgh", "c_onesrow"), "psb6", start=False, stop=True)
                ACT(gts[:, ch * 128:(ch + 1) * 128], psb[6][0:128, 0:128],
                    ACTF.Silu, ("psb6",), ("gts",))
            DMA("scalar", dsl(sgate, h, 128, SEG), gts[:], ("gts",),
                ("w_sg",))
        ar1.free()

        pr.barrier()

        ar2 = bass.AutoArena(nc)
        q = ar2.sbuf_tensor("s_q", [128, SEG], BF16)
        k = ar2.sbuf_tensor("s_k", [128, SEG], BF16)
        vb = ar2.sbuf_tensor("s_vbt", [128, SEG], BF16)
        g = ar2.sbuf_tensor("s_g", [128, SEG], F32)
        brs = ar2.sbuf_tensor("s_brs", [128, SEG], BF16)
        kb = ar2.sbuf_tensor("s_kb", [128, SEG], BF16)
        Dp = ar2.sbuf_tensor("s_Dp", [128, SEG], F32)
        Dq = ar2.sbuf_tensor("s_Dq", [128, SEG], F32)
        Rp = ar2.sbuf_tensor("s_Rp", [128, SEG], F32)
        Rq = ar2.sbuf_tensor("s_Rq", [128, SEG], F32)
        ETp = ar2.sbuf_tensor("s_ETp", [128, SEG], BF16)
        ETq = ar2.sbuf_tensor("s_ETq", [128, SEG], BF16)
        EAp = ar2.sbuf_tensor("s_EAp", [128, SEG], BF16)
        EAq = ar2.sbuf_tensor("s_EAq", [128, SEG], BF16)
        FTp = ar2.sbuf_tensor("s_FTp", [128, SEG], BF16)
        FTq = ar2.sbuf_tensor("s_FTq", [128, SEG], BF16)
        FAp = ar2.sbuf_tensor("s_FAp", [128, SEG], BF16)
        FAq = ar2.sbuf_tensor("s_FAq", [128, SEG], BF16)
        kbD = {l: ar2.sbuf_tensor(f"s_kbD{l}", [128, SEG], BF16)
               for l in LEVELS}
        qD = {l: ar2.sbuf_tensor(f"s_qD{l}", [128, SEG], BF16)
              for l in LEVELS}
        kR = {l: ar2.sbuf_tensor(f"s_kR{l}", [128, SEG], BF16)
              for l in LEVELS}
        Ktil = ar2.sbuf_tensor("s_Ktil", [128, SEG], BF16)
        Qtil = ar2.sbuf_tensor("s_Qtil", [128, SEG], BF16)
        Khat = ar2.sbuf_tensor("s_Khat", [128, SEG], BF16)
        GammaC = ar2.sbuf_tensor("s_GammaC", [128, NCH], F32)
        dcm = ar2.sbuf_tensor("s_dcm", [32, SEG], BF16)
        dtm = ar2.sbuf_tensor("s_dtm", [128, NCH * 18], BF16)
        btm = ar2.sbuf_tensor("s_btm", [128, NCH], F32)
        bandTF = ar2.sbuf_tensor("s_bandTF", [128, SEG], BF16)
        bandAF = ar2.sbuf_tensor("s_bandAF", [128, SEG], BF16)
        Amat = {c_: ar2.sbuf_tensor(f"s_Amat{c_}", [128, 128], BF16)
                for c_ in range(NCH)}
        mats = []
        for p_ in range(2):
            mats.append(dict(
                ck=f"s_comp{p_}",
                Pt=ar2.sbuf_tensor(f"s_Pt{p_}", [128, 128], BF16),
                Pn=ar2.sbuf_tensor(f"s_Pn{p_}", [128, 128], BF16),
                comp=ar2.sbuf_tensor(f"s_comp{p_}", [128, 128], BF16),
                Qpow={j: ar2.sbuf_tensor(f"s_Qpow{p_}_{j}", [128, 128],
                                         BF16) for j in range(1, 6)},
                Ppw={j: ar2.sbuf_tensor(f"s_Ppw{p_}_{j}", [128, 128],
                                        BF16) for j in range(1, 6)},
                IQ={j: ar2.sbuf_tensor(f"s_IQ{p_}_{j}", [128, 128],
                                       BF16) for j in range(0, 7)}))
        S = ar2.sbuf_tensor("s_S", [128, DVP], F32)
        Sbf = ar2.sbuf_tensor("s_Sbf", [128, DVP], BF16)
        Ktm = ar2.sbuf_tensor("s_Ktm", [128, 128], BF16)
        vtm = ar2.sbuf_tensor("s_vtm", [128, 128], BF16)
        rr = ar2.sbuf_tensor("s_rr", [128, DVP], BF16)
        ww = ar2.sbuf_tensor("s_ww", [128, DVP], BF16)
        oloc = ar2.sbuf_tensor("s_oloc", [128, 512], BF16)
        rcml = ar2.sbuf_tensor("s_rcml", [128, 512], BF16)
        sfinb = ar2.sbuf_tensor("s_sfinb", [128, DVP], BF16)
        ar2.allocate()

        def blk(t_, lvl, half):
            r = t_.rearrange("p (n b) -> p n b", b=2 * lvl)
            return r[:, :, 0:lvl] if half == "lo" else r[:, :, lvl:2 * lvl]

        def bcol(t_, lvl, col, cnt):
            r = t_.rearrange("p (n b) -> p n b", b=2 * lvl)
            return r[:, :, col:col + 1].broadcast_to(
                (128, SEG // (2 * lvl), cnt))

        MS(kbD[8][:, 0:1032], 0.0, ("s_kbD8",))
        for r_ in range(4):
            DMA("sync", bass.AP(shear.tensor, r_ * 132096,
                                [[1032, 128], [1, 1032]]),
                kbD[8][:, 0:1032], ("s_kbD8",), ("shearbuf",))
        for t_, k_ in ((FTp, "s_FTd"), (FTq, "s_FTd"), (FAp, "s_FAd"),
                       (FAq, "s_FAd"), (ETq, "s_ETq"), (EAq, "s_EAq")):
            MS(t_[:], 0.0, (k_,))

        band_banks = ["psb0", "psb1", "psb2", "psb3"]

        def band_mms(Ftile, Fkey, slot, first, last=False):
            for n in range(NT4):
                MM(psb[n][0:18, 0:512], ohbig[:, slot * 18:(slot + 1) * 18],
                   Ftile[:, n * 512:(n + 1) * 512],
                   (Fkey, "c_ohbig"), band_banks[n], start=first, stop=last)

        def emit_nodes(cc, bank_t, bank_key, rhsF, rhsP, M):
            base = cc * C
            MS(psb[6][0:128, 0:128], 0.0, ("psb6",))
            for j in (1, 3, 5, 7):
                crow = 16 * j - 16
                tp = (0, 96) if crow == 96 else None
                MM(psb[6][crow:crow + 8, 16 * j + 8:16 * j + 16],
                   kR[8][:, base + 16 * j:base + 16 * j + 8],
                   rhsF[8][:, base + 16 * j + 8:base + 16 * j + 16],
                   ("s_kR8", f"{rhsP}8"), "psb6", start=True, stop=True,
                   tp=tp)
            ACP(M["comp"][:], psb[6][0:128, 0:128], ("psb6",), (M["ck"],))
            MM(bank_t[0:128, 0:128], shmA[:], M["comp"][:],
               ("c_shA", M["ck"]), bank_key, start=True, stop=False)

            def nmm(rlo, clo, m_, n_, lvl_):
                tp = (0, 96) if rlo == 96 else None
                MM(bank_t[rlo:rlo + m_, clo:clo + n_],
                   kR[lvl_][:, base + rlo:base + rlo + m_],
                   rhsF[lvl_][:, base + clo:base + clo + n_],
                   (f"s_kR{lvl_}", f"{rhsP}{lvl_}"), bank_key,
                   start=False, stop=False, tp=tp)

            nmm(0, 64, 64, 64, 64)
            nmm(0, 32, 32, 32, 32)
            nmm(64, 96, 32, 32, 32)
            for i in range(4):
                nmm(32 * i, 32 * i + 16, 16, 16, 16)
            for j in (0, 2, 4, 6):
                nmm(16 * j, 16 * j + 8, 8, 8, 8)
            MM(bank_t[0:128, 0:128], shmB[:], M["comp"][:],
               ("c_shB", M["ck"]), bank_key, start=False, stop=True)

        for h in range(H):
            DMA("sync", q[:], dsl(fq, h, 128, SEG), ("w_fql", "s_q"),
                ("s_q",))
            DMA("sync", k[:], dsl(fk, h, 128, SEG), ("w_fql", "s_k"),
                ("s_k",))
            DMA("sync", vb[:], dsl(fv, h, 128, SEG), ("w_fv", "s_vbt"),
                ("s_vbt",))
            DMA("sync", g[:], dsl(fg, h, 128, SEG), ("w_fg", "s_g"),
                ("s_g",))
            DMA("sync", brs[:], dsl(fbb, h, 128, SEG), ("w_fbb", "s_brs"),
                ("s_brs",))
            TT(kb[:], k[:], brs[:], ALU.mult, ("s_k", "s_brs"), ("s_kb",))
            MS(S[:, 0:128], 0.0, ("s_S",))
            VCP(S[:, 128:256], ident[:], ("c_ident",), ("s_S",))

            VCP(blk(Dp, 1, "lo"), blk(g, 1, "lo"), ("s_g",), ("s_Dp",))
            TT(blk(Dp, 1, "hi"), blk(g, 1, "hi"), blk(g, 1, "lo"),
               ALU.mult, ("s_g",), ("s_Dp",))
            MS(Rp[:], 1.0, ("s_Rp",))
            VCP(blk(Rp, 1, "lo"), blk(g, 1, "hi"), ("s_g",), ("s_Rp",))
            Dc, Dn_, Dk_, Dnk = Dp, Dq, "s_Dp", "s_Dq"
            Rc, Rn_, Rk_, Rnk = Rp, Rq, "s_Rp", "s_Rq"
            lvl = 2
            while True:
                if lvl in LEVELS:
                    TT(kbD[lvl][:], kb[:], Dc[:], ALU.mult,
                       ("s_kb", Dk_), (f"s_kbD{lvl}",))
                    TT(qD[lvl][:], q[:], Dc[:], ALU.mult,
                       ("s_q", Dk_), (f"s_qD{lvl}",))
                    TT(kR[lvl][:], k[:], Rc[:], ALU.mult,
                       ("s_k", Rk_), (f"s_kR{lvl}",))
                if lvl == 128:
                    VCP(GammaC[:], Dc.rearrange(
                        "p (n b) -> p n b", b=C)[:, :, C - 1:C].squeeze(2),
                        (Dk_,), ("s_GammaC",))
                    TT(Ktil[:], k[:], Dc[:], ALU.mult, ("s_k", Dk_),
                       ("s_Ktil",))
                    TT(Qtil[:], q[:], Dc[:], ALU.mult, ("s_q", Dk_),
                       ("s_Qtil",))
                    TT(Khat[:], k[:], Rc[:], ALU.mult, ("s_k", Rk_),
                       ("s_Khat",))
                    break
                l = lvl
                VCP(blk(Rn_, l, "hi"), blk(Rc, l, "hi"), (Rk_,), (Rnk,))
                TT(blk(Rn_, l, "lo"), blk(Rc, l, "lo"),
                   bcol(Dc, l, 2 * l - 1, l), ALU.mult, (Rk_, Dk_), (Rnk,))
                Rc, Rn_, Rk_, Rnk = Rn_, Rc, Rnk, Rk_
                VCP(blk(Dn_, l, "lo"), blk(Dc, l, "lo"), (Dk_,), (Dnk,))
                TT(blk(Dn_, l, "hi"), blk(Dc, l, "hi"),
                   bcol(Dc, l, l - 1, l), ALU.mult, (Dk_,), (Dnk,))
                Dc, Dn_, Dk_, Dnk = Dn_, Dc, Dnk, Dk_
                lvl *= 2

            TT(ETp[:], kb[:], g[:], ALU.mult, ("s_kb", "s_g"), ("s_ETp",))
            TT(EAp[:], q[:], g[:], ALU.mult, ("s_q", "s_g"), ("s_EAp",))
            TT(FAp[:], q[:], k[:], ALU.mult, ("s_q", "s_k"), ("s_FAd",))
            band_mms(FAp, "s_FAd", 8, True)
            ET, ETn, ETk, ETnk = ETp, ETq, "s_ETp", "s_ETq"
            EA, EAn, EAk, EAnk = EAp, EAq, "s_EAp", "s_EAq"
            FTc, FTn = FTp, FTq
            FAc, FAn = FAq, FAp
            for d in range(1, 8):
                TT(FTc[:, 0:SEG - d], ET[:, d:SEG], k[:, 0:SEG - d],
                   ALU.mult, (ETk, "s_k"), ("s_FTd",))
                band_mms(FTc, "s_FTd", d, False)
                TT(FAc[:, 0:SEG - d], EA[:, d:SEG], k[:, 0:SEG - d],
                   ALU.mult, (EAk, "s_k"), ("s_FAd",))
                band_mms(FAc, "s_FAd", 8 + d, False, last=(d == 7))
                FTc, FTn = FTn, FTc
                FAc, FAn = FAn, FAc
                if d < 7:
                    TT(ETn[:, d:SEG], ET[:, d:SEG], g[:, 0:SEG - d],
                       ALU.mult, (ETk, "s_g"), (ETnk,))
                    TT(EAn[:, d:SEG], EA[:, d:SEG], g[:, 0:SEG - d],
                       ALU.mult, (EAk, "s_g"), (EAnk,))
                    ET, ETn, ETk, ETnk = ETn, ET, ETnk, ETk
                    EA, EAn, EAk, EAnk = EAn, EA, EAnk, EAk
            for n in range(NT4):
                VCP(dcm[0:18, n * 512:(n + 1) * 512], psb[n][0:18, 0:512],
                    (band_banks[n],), ("s_dcm",))
            DMA("gpsimd", dcm[16:17, :], brs[0:1, :], ("s_brs", "s_dcm"),
                ("s_dcm",))
            for cc in range(NCH):
                TPo(pstp[0:128, 0:18], dcm[0:18, cc * 128:(cc + 1) * 128],
                    ident[0:18, 0:18], ("s_dcm",), "pstpW0")
                VCP(dtm[:, cc * 18:(cc + 1) * 18], pstp[0:128, 0:18],
                    ("pstpW0",), ("s_dtm",))
                TSm(btm[:, cc:cc + 1], dtm[:, cc * 18 + 16:cc * 18 + 17],
                    -1.0, ("s_dtm",), ("s_btm",))
            TT(dtm.rearrange("p (c e) -> p c e", e=18)[:, :, 0:16],
               dtm.rearrange("p (c e) -> p c e", e=18)[:, :, 0:16],
               bmask[:, 0:16].unsqueeze(1).broadcast_to((128, NCH, 16)),
               ALU.mult, ("s_dtm", "c_bandmask"), ("s_dtm",))
            for m_ in range(2):
                DMA("sync", bass.AP(shear.tensor, m_ * 264192,
                                    [[129, 128], [16512, NCH], [1, 8]]),
                    bass.AP(dtm, m_ * 8, [[NCH * 18, 128], [18, NCH],
                                          [1, 8]]),
                    ("s_dtm", "shearbuf"), ("shearbuf",))
            DMA("sync", bass.AP(bandTF, 0, [[SEG, 128], [128, NCH],
                                            [1, 128]]),
                bass.AP(shear.tensor, 0, [[128, 128], [16512, NCH],
                                          [1, 128]]),
                ("shearbuf", "s_bandTF"), ("s_bandTF",))
            DMA("sync", bass.AP(bandAF, 0, [[SEG, 128], [128, NCH],
                                            [1, 128]]),
                bass.AP(shear.tensor, 264192, [[128, 128], [16512, NCH],
                                               [1, 128]]),
                ("shearbuf", "s_bandAF"), ("s_bandAF",))

            for cc in range(NCH):
                base = cc * C
                p_ = cc % 2
                M = mats[p_]
                bT, bTk = psb[2 * p_], f"psb{2 * p_}"
                bA, bAk = psb[2 * p_ + 1], f"psb{2 * p_ + 1}"
                emit_nodes(cc, bT, bTk, kbD, "s_kbD", M)
                STT(M["Pt"][:], bT[0:128, 0:128], -1.0,
                    bandTF[:, base:base + C], ALU.mult, ALU.add,
                    (bTk, "s_bandTF"), (f"s_Pt{p_}",))
                emit_nodes(cc, bA, bAk, qD, "s_qD", M)
                STT(Amat[cc][:], bA[0:128, 0:128], 1.0,
                    bandAF[:, base:base + C], ALU.mult, ALU.add,
                    (bAk, "s_bandAF"), (f"s_Amat{cc}",))
                tw = "pstpW1" if p_ == 0 else "pstpW3"
                twc = 256 if p_ == 0 else 768
                TPo(pstp[0:128, twc:twc + 128], M["Pt"][:], ident[:],
                    (f"s_Pt{p_}",), tw)
                ACP(M["Pn"][:], pstp[0:128, twc:twc + 128], (tw,),
                    (f"s_Pn{p_}",))
                TT(M["IQ"][0][:], M["Pt"][:], ident[:], ALU.add,
                   (f"s_Pt{p_}", "c_ident"), (f"s_IQ{p_}_0",))
                prevQ, prevQk = M["Pt"], f"s_Pt{p_}"
                prevP, prevPk = M["Pn"], f"s_Pn{p_}"
                for j in range(1, 7):
                    MM(psb[6][0:128, 0:128], prevP[:], prevQ[:],
                       (prevPk, prevQk), "psb6", start=True, stop=True)
                    if j < 6:
                        ACP(M["Qpow"][j][:], psb[6][0:128, 0:128],
                            ("psb6",), (f"s_Qpow{p_}_{j}",))
                        TT(M["IQ"][j][:], M["Qpow"][j][:], ident[:],
                           ALU.add, (f"s_Qpow{p_}_{j}", "c_ident"),
                           (f"s_IQ{p_}_{j}",))
                        TPo(pstp[0:128, twc:twc + 128], M["Qpow"][j][:],
                            ident[:], (f"s_Qpow{p_}_{j}",), tw)
                        ACP(M["Ppw"][j][:], pstp[0:128, twc:twc + 128],
                            (tw,), (f"s_Ppw{p_}_{j}",))
                        prevQ, prevQk = M["Qpow"][j], f"s_Qpow{p_}_{j}"
                        prevP, prevPk = M["Ppw"][j], f"s_Ppw{p_}_{j}"
                    else:
                        TT(M["IQ"][j][:], psb[6][0:128, 0:128], ident[:],
                           ALU.add, ("psb6", "c_ident"), (f"s_IQ{p_}_{j}",))
                TPo(pstp[0:128, 512:640], Khat[:, base:base + C], ident[:],
                    ("s_Khat",), "pstpW2")
                ACP(Ktm[:], pstp[0:128, 512:640], ("pstpW2",), ("s_Ktm",))
                TPo(pstp[0:128, 640:768], vb[:, base:base + C], ident[:],
                    ("s_vbt",), "pstpW2")
                ACP(vtm[:], pstp[0:128, 640:768], ("pstpW2",), ("s_vtm",))
                ACP(Sbf[:], S[:], ("s_S",), ("s_Sbf",))
                MM(psb[4][0:128, 0:DVP], Ktil[:, base:base + C], Sbf[:],
                   ("s_Ktil", "s_Sbf"), "psb4", start=True, stop=True)
                STT(rr[:, 0:128], psb[4][0:128, 0:128], btm[:, cc:cc + 1],
                    vtm[:], ALU.mult, ALU.add,
                    ("psb4", "s_btm", "s_vtm"), ("s_rr",))
                E("vector", (lambda cc_: lambda v: v.tensor_scalar_mul(
                    rr[:, 128:256], psb[4][0:128, 128:256],
                    btm[:, cc_:cc_ + 1]))(cc),
                  reads=("psb4", "s_btm"), writes=("s_rr",))
                wcur, wk2 = rr, "s_rr"
                for j in (6, 5, 4, 3, 2, 1, 0):
                    MM(psb[4][0:128, 0:DVP], M["IQ"][j][:], wcur[:],
                       (f"s_IQ{p_}_{j}", wk2), "psb4", start=True,
                       stop=True)
                    dst, dk2 = (ww, "s_ww") if wcur is rr else (rr, "s_rr")
                    ACP(dst[:], psb[4][0:128, 0:DVP], ("psb4",), (dk2,))
                    wcur, wk2 = dst, dk2
                MM(psb[5][0:128, 0:128], Qtil[:, base:base + C],
                   Sbf[:, 0:128], ("s_Qtil", "s_Sbf"), "psb5",
                   start=True, stop=False)
                MM(psb[5][0:128, 0:128], Amat[cc][:], wcur[:, 0:128],
                   (f"s_Amat{cc}", wk2), "psb5", start=False, stop=True)
                MM(psb[5][0:128, 128:256], Sbf[:, 128:256],
                   Qtil[:, base:base + C], ("s_Qtil", "s_Sbf"), "psb5",
                   start=True, stop=False)
                MM(psb[5][0:128, 128:256], wcur[:, 128:256], Amat[cc][:],
                   (f"s_Amat{cc}", wk2), "psb5", start=False, stop=True)
                q4 = cc % 4
                VCP(oloc[:, q4 * 128:(q4 + 1) * 128], psb[5][0:128, 0:128],
                    ("psb5",), ("s_oloc",))
                VCP(rcml[:, q4 * 128:(q4 + 1) * 128],
                    psb[5][0:128, 128:256], ("psb5",), ("s_rcml",))
                if q4 == 3:
                    c0 = cc - 3
                    DMA("scalar", bass.AP(obuf.tensor,
                                          h * 128 * SEG + c0 * 128,
                                          [[SEG, 128], [128, 4], [1, 128]]),
                        oloc[:], ("s_oloc", "obuf"), ("obuf",))
                    DMA("scalar", bass.AP(rbuf.tensor,
                                          h * 128 * SEG + c0 * 128,
                                          [[SEG, 128], [128, 4], [1, 128]]),
                        rcml[:], ("s_rcml", "respbuf"), ("respbuf",))
                MM(psb[4][0:128, 256:512], Ktm[:], wcur[:],
                   ("s_Ktm", wk2), "psb4", start=True, stop=True)
                STT(S[:], S[:], GammaC[:, cc:cc + 1],
                    psb[4][0:128, 256:512], ALU.mult, ALU.add,
                    ("s_S", "s_GammaC", "psb4"), ("s_S",))
            ACP(sfinb[:], S[:], ("s_S",), ("s_sfinb",))
            TPo(pstp[0:128, 0:128], sfinb[:, 128:256], ident[:],
                ("s_sfinb",), "pstpW0")
            ACP(sfinb[:, 128:256], pstp[0:128, 0:128], ("pstpW0",),
                ("s_sfinb",))
            DMA("sync", agin[h * 128:(h + 1) * 128, :], sfinb[:],
                ("s_sfinb", "aginbuf"), ("aginbuf",))

        E("gpsimd", lambda gp: gp.collective_compute(
            "AllGather", ALU.bypass, replica_groups=[list(range(8))],
            ins=[agin.opt()], outs=[agout.opt()]),
          reads=("aginbuf",), writes=("agoutbuf",))
        ar2.free()

        pr.barrier()

        ar3 = bass.AutoArena(nc)
        eph = []
        for p_ in range(2):
            eph.append(dict(
                agS=ar3.sbuf_tensor(f"e_agS{p_}", [128, 8 * DVP], BF16),
                Sacc=ar3.sbuf_tensor(f"e_Sacc{p_}", [128, 128], F32),
                Saccb=ar3.sbuf_tensor(f"e_Saccb{p_}", [128, 128], BF16),
                t1e=ar3.sbuf_tensor(f"e_t1e{p_}", [128, 128], F32),
                oLh=ar3.sbuf_tensor(f"e_oLh{p_}", [128, SEG], BF16),
                rCh=ar3.sbuf_tensor(f"e_rCh{p_}", [128, SEG], BF16),
                gth=ar3.sbuf_tensor(f"e_gth{p_}", [128, SEG], BF16),
                ofin=ar3.sbuf_tensor(f"e_ofin{p_}", [128, 128], F32),
                osq=ar3.sbuf_tensor(f"e_osq{p_}", [128, 128], F32),
                onrm=ar3.sbuf_tensor(f"e_onrm{p_}", [128, 1], F32),
                ont=ar3.sbuf_tensor(f"e_ont{p_}", [128, 128], BF16)))
        onc = {hh: ar3.sbuf_tensor(f"e_onc{hh}", [128, SEG], BF16)
               for hh in range(H)}
        wo2 = {i: ar3.sbuf_tensor(f"e_wo2{i}", [128, 2048], BF16)
               for i in range(2)}
        x2t = ar3.sbuf_tensor("e_x2t", [128, SEG], F32)
        xres = ar3.sbuf_tensor("e_xres", [128, SEG], F32)
        ar3.allocate()

        for h in range(H):
            p_ = h % 2
            ep = eph[p_]
            kk = {nm: f"e_{nm}{p_}" for nm in
                  ("agS", "Sacc", "Saccb", "t1e", "oLh", "rCh", "gth",
                   "ofin", "osq", "onrm", "ont")}
            DMA("sync", ep["agS"][:],
                bass.AP(agout.tensor, h * 128 * DVP,
                        [[DVP, 128], [H * 128 * DVP, 8], [1, DVP]]),
                ("agoutbuf", kk["agS"]), (kk["agS"],))
            MS(ep["Sacc"][:], 0.0, (kk["Sacc"],))
            for j in range(8):
                ACP(ep["Saccb"][:], ep["Sacc"][:], (kk["Sacc"],),
                    (kk["Saccb"],))
                MM(psb[p_][0:128, 0:128],
                   ep["agS"][:, j * DVP + 128:(j + 1) * DVP],
                   ep["Saccb"][:], (kk["agS"], kk["Saccb"]), f"psb{p_}",
                   start=True, stop=True)
                TT(ep["t1e"][:], psb[p_][0:128, 0:128],
                   ep["agS"][:, j * DVP:j * DVP + 128], ALU.add,
                   (f"psb{p_}", kk["agS"]), (kk["t1e"],))
                TT(ep["t1e"][:], ep["t1e"][:], ep["Sacc"][:], ALU.subtract,
                   (kk["t1e"], kk["Sacc"]), (kk["t1e"],))
                STT(ep["Sacc"][:], ep["t1e"][:], gam[:, j:j + 1],
                    ep["Sacc"][:], ALU.mult, ALU.add,
                    (kk["t1e"], "c_gam", kk["Sacc"]), (kk["Sacc"],))
            ACP(ep["Saccb"][:], ep["Sacc"][:], (kk["Sacc"],),
                (kk["Saccb"],))
            DMA("sync", ep["oLh"][:], dsl(obuf, h, 128, SEG),
                ("obuf", kk["oLh"]), (kk["oLh"],))
            DMA("sync", ep["rCh"][:], dsl(rbuf, h, 128, SEG),
                ("respbuf", kk["rCh"]), (kk["rCh"],))
            DMA("sync", ep["gth"][:], dsl(sgate, h, 128, SEG),
                ("w_sg", kk["gth"]), (kk["gth"],))
            for cc in range(NCH):
                MM(psb[2 + p_][0:128, 0:128],
                   ep["rCh"][:, cc * 128:(cc + 1) * 128], ep["Saccb"][:],
                   (kk["rCh"], kk["Saccb"]), f"psb{2 + p_}",
                   start=True, stop=True)
                TT(ep["ofin"][:], psb[2 + p_][0:128, 0:128],
                   ep["oLh"][:, cc * 128:(cc + 1) * 128], ALU.add,
                   (f"psb{2 + p_}", kk["oLh"]), (kk["ofin"],))
                ACT(ep["osq"][:], ep["ofin"][:], ACTF.Square,
                    (kk["ofin"],), (kk["osq"],))
                E("vector", (lambda ep_: lambda v: v.tensor_reduce(
                    ep_["onrm"][:],
                    ep_["osq"].rearrange("p (a b) -> p a b", b=128),
                    AX.X, ALU.add))(ep),
                  reads=(kk["osq"],), writes=(kk["onrm"],))
                ACT(ep["onrm"][:], ep["onrm"][:], ACTF.Sqrt,
                    (kk["onrm"],), (kk["onrm"],), bias=1e-5,
                    scale=1.0 / 128.0)
                RCP(ep["onrm"][:], (kk["onrm"],), (kk["onrm"],))
                STT(ep["ont"][:], ep["ofin"][:], ep["onrm"][:, 0:1],
                    onw[:], ALU.mult, ALU.mult,
                    (kk["ofin"], kk["onrm"], "c_onw"), (kk["ont"],))
                TT(ep["ont"][:], ep["ont"][:],
                   ep["gth"][:, cc * 128:(cc + 1) * 128], ALU.mult,
                   (kk["ont"], kk["gth"]), (kk["ont"],))
                tw = "pstpW1" if p_ == 0 else "pstpW3"
                twc = 256 if p_ == 0 else 768
                TPo(pstp[0:128, twc:twc + 128], ep["ont"][:], ident[:],
                    (kk["ont"],), tw)
                VCP(onc[h][:, cc * 128:(cc + 1) * 128],
                    pstp[0:128, twc:twc + 128], (tw,), (f"e_onc{h}",))
        for dt_ in range(NDT):
            wo = wo2[dt_ % 2]
            wok = f"e_wo2{dt_ % 2}"
            DMA("gpsimd", wo[:],
                bass.AP(dr["WoP"].tensor, dt_ * 128,
                        [[D, 128], [128 * D, 16], [1, 128]]),
                (wok,), (wok,))
            for n in range(NT4):
                bi = 4 + n % 3
                bk = f"psb{bi}"
                for h in range(H):
                    MM(psb[bi][0:128, 0:512], wo[:, h * 128:(h + 1) * 128],
                       onc[h][:, n * 512:(n + 1) * 512],
                       (f"e_onc{h}", wok), bk,
                       start=(h == 0), stop=(h == H - 1))
                VCP(x2t[:, n * 512:(n + 1) * 512], psb[bi][0:128, 0:512],
                    (bk,), ("e_x2t",))
            DMA("sync", xres[:], dr["xT"][dt_ * 128:(dt_ + 1) * 128, :],
                ("e_xres",), ("e_xres",))
            TT(x2t[:], x2t[:], xres[:], ALU.add, ("e_x2t", "e_xres"),
               ("e_x2t",))
            DMA("scalar", x2buf[dt_ * 128:(dt_ + 1) * 128, :], x2t[:],
                ("e_x2t", "x2buf"), ("x2buf",))
        ar3.free()

        pr.barrier()

        ar4 = bass.AutoArena(nc)
        xf = {i: ar4.sbuf_tensor(f"f_xf{i}", [128, SEG + 3], BF16)
              for i in range(NDT)}
        ln2 = {"xs": ar4.sbuf_tensor("f_xs", [128, SEG], F32),
               "smrow": ar4.sbuf_tensor("f_smrow", [1, SEG], F32),
               "sqrow": ar4.sbuf_tensor("f_sqrow", [1, SEG], F32),
               "mrb": ar4.sbuf_tensor("f_mrb", [1, SEG], BF16),
               "ivb": ar4.sbuf_tensor("f_ivb", [1, SEG], BF16),
               "murepb": ar4.sbuf_tensor("f_murepb", [128, SEG], BF16),
               "ivrepb": ar4.sbuf_tensor("f_ivrepb", [128, SEG], BF16)}
        sqs = ar4.sbuf_tensor("f_sqs", [128, SEG], BF16)
        gw = {i: ar4.sbuf_tensor(f"f_gw{i}", [128, 2048], BF16)
              for i in range(2)}
        uw = {i: ar4.sbuf_tensor(f"f_uw{i}", [128, 2048], BF16)
              for i in range(2)}
        gsc = {i: ar4.sbuf_tensor(f"f_gsc{i}", [128, 1], F32)
               for i in range(2)}
        usc = {i: ar4.sbuf_tensor(f"f_usc{i}", [128, 1], F32)
               for i in range(2)}
        gsil = ar4.sbuf_tensor("f_gsil", [128, SEG], BF16)
        uup = ar4.sbuf_tensor("f_uup", [128, SEG], BF16)
        hht = ar4.sbuf_tensor("f_hht", [128, SEG], BF16)
        sxrow = ar4.sbuf_tensor("f_sxrow", [1, SEG], F32)
        sxh = ar4.sbuf_tensor("f_sxh", [1, SEG], BF16)
        sxl = ar4.sbuf_tensor("f_sxl", [1, SEG], BF16)
        shr = ar4.sbuf_tensor("f_shr", [1, SEG], F32)
        shh = ar4.sbuf_tensor("f_shh", [1, SEG], BF16)
        shl = ar4.sbuf_tensor("f_shl", [1, SEG], BF16)
        ar4.allocate()

        emit_layernorm("f", ln2, x2buf, "x2buf", xf, lambda d: f"xf{d}",
                       sqs, "f_sqs")
        for n in range(NT4):
            for dt_ in range(NDT):
                MM(psb[n][0:1, 0:512], onescol[:, 0:1],
                   xf[dt_][:, 3 + n * 512:3 + (n + 1) * 512],
                   (f"xf{dt_}", "c_onescol"), f"psb{n}",
                   start=(dt_ == 0), stop=(dt_ == NDT - 1))
            VCP(sxrow[0:1, n * 512:(n + 1) * 512], psb[n][0:1, 0:512],
                (f"psb{n}",), ("f_sxrow",))
        ACP(sxh[:], sxrow[:], ("f_sxrow",), ("f_sxh",))
        TT(sxrow[:], sxrow[:], sxh[:], ALU.subtract, ("f_sxrow", "f_sxh"),
           ("f_sxrow",))
        ACP(sxl[:], sxrow[:], ("f_sxrow",), ("f_sxl",))
        for it in range(NIT):
            sl = it % 2
            DMA("sync", gw[sl][:],
                bass.AP(dr["gwP"].tensor, it * D * 128,
                        [[128, 128], [128 * 128, 16], [1, 128]]),
                (f"f_gw{sl}",), (f"f_gw{sl}",))
            DMA("gpsimd", uw[sl][:],
                bass.AP(dr["uwP"].tensor, it * D * 128,
                        [[128, 128], [128 * 128, 16], [1, 128]]),
                (f"f_uw{sl}",), (f"f_uw{sl}",))
            DMA("gpsimd", gsc[sl][:],
                bass.AP(dr["gsP"].tensor, it * 128, [[1, 128], [1, 1]]),
                (f"f_gsc{sl}",), (f"f_gsc{sl}",))
            DMA("gpsimd", usc[sl][:],
                bass.AP(dr["usP"].tensor, it * 128, [[1, 128], [1, 1]]),
                (f"f_usc{sl}",), (f"f_usc{sl}",))
            for n in range(NT4):
                for dt_ in range(NDT):
                    MM(psb[0][0:128, 0:512],
                       gw[sl][:, dt_ * 128:(dt_ + 1) * 128],
                       xf[dt_][:, 3 + n * 512:3 + (n + 1) * 512],
                       (f"xf{dt_}", f"f_gw{sl}"), "psb0",
                       start=(dt_ == 0), stop=False)
                MM(psb[0][0:128, 0:512], c128row[:],
                   sxh[0:1, n * 512:(n + 1) * 512],
                   ("c_c128row", "f_sxh"), "psb0", start=False, stop=False)
                MM(psb[0][0:128, 0:512], c128row[:],
                   sxl[0:1, n * 512:(n + 1) * 512],
                   ("c_c128row", "f_sxl"), "psb0", start=False, stop=True)
                E("scalar", (lambda n_, sl_: lambda s: s.activation(
                    gsil[:, n_ * 512:(n_ + 1) * 512], psb[0][0:128, 0:512],
                    ACTF.Silu, scale=gsc[sl_][:, 0:1]))(n, sl),
                  reads=("psb0", f"f_gsc{sl}"), writes=("f_gsil",))
                for dt_ in range(NDT):
                    MM(psb[1][0:128, 0:512],
                       uw[sl][:, dt_ * 128:(dt_ + 1) * 128],
                       xf[dt_][:, 3 + n * 512:3 + (n + 1) * 512],
                       (f"xf{dt_}", f"f_uw{sl}"), "psb1",
                       start=(dt_ == 0), stop=False)
                MM(psb[1][0:128, 0:512], c128row[:],
                   sxh[0:1, n * 512:(n + 1) * 512],
                   ("c_c128row", "f_sxh"), "psb1", start=False, stop=False)
                MM(psb[1][0:128, 0:512], c128row[:],
                   sxl[0:1, n * 512:(n + 1) * 512],
                   ("c_c128row", "f_sxl"), "psb1", start=False, stop=True)
                E("scalar", (lambda n_, sl_: lambda s: s.activation(
                    uup[:, n_ * 512:(n_ + 1) * 512], psb[1][0:128, 0:512],
                    ACTF.Copy, scale=usc[sl_][:, 0:1]))(n, sl),
                  reads=("psb1", f"f_usc{sl}"), writes=("f_uup",))
            TT(hht[:], gsil[:], uup[:], ALU.mult, ("f_gsil", "f_uup"),
               ("f_hht",))
            DMA("scalar", dsl(hhbuf, it, 128, SEG), hht[:],
                ("f_hht", "hhbuf"), ("hhbuf",))
            for n in range(NT4):
                MM(psb[2 + n][0:1, 0:512], onescol[:, 0:1],
                   hht[:, n * 512:(n + 1) * 512],
                   ("f_hht", "c_onescol"), f"psb{2 + n}",
                   start=(it == 0), stop=(it == NIT - 1))
        for n in range(NT4):
            VCP(shr[0:1, n * 512:(n + 1) * 512], psb[2 + n][0:1, 0:512],
                (f"psb{2 + n}",), ("f_shr",))
        ACP(shh[:], shr[:], ("f_shr",), ("f_shh",))
        TT(shr[:], shr[:], shh[:], ALU.subtract, ("f_shr", "f_shh"),
           ("f_shr",))
        ACP(shl[:], shr[:], ("f_shr",), ("f_shl",))
        DMA("sync", shbuf[0:1, :], shh[:], ("f_shh", "shbuf"), ("shbuf",))
        DMA("sync", shbuf[1:2, :], shl[:], ("f_shl", "shbuf"), ("shbuf",))
        ar4.free()

        pr.barrier()

        ar5 = bass.AutoArena(nc)
        hh2 = ar5.sbuf_tensor("b_hh2", [128, NIT * 1024], BF16)
        dwb = {i: ar5.sbuf_tensor(f"b_dwb{i}", [128, NIT * 128], BF16)
               for i in range(2)}
        x2s = {i: ar5.sbuf_tensor(f"b_x2s{i}", [128, 1024], F32)
               for i in range(2)}
        yt = {i: ar5.sbuf_tensor(f"b_yt{i}", [128, 512], F32)
              for i in range(2)}
        dsc = ar5.sbuf_tensor("b_dsc", [128, 16], F32)
        shh2 = ar5.sbuf_tensor("b_shh2", [1, SEG], BF16)
        shl2 = ar5.sbuf_tensor("b_shl2", [1, SEG], BF16)
        ar5.allocate()

        DMA("sync", dsc[:], dr["dsP"], ("b_dsc",), ("b_dsc",))
        DMA("sync", shh2[:], shbuf[0:1, :], ("shbuf", "b_shh2"),
            ("b_shh2",))
        DMA("sync", shl2[:], shbuf[1:2, :], ("shbuf", "b_shl2"),
            ("b_shl2",))
        for th in range(2):
            for ib in range(8):
                DMA("sync", hh2[:, ib * 8192:(ib + 1) * 8192],
                    bass.AP(hhbuf.tensor, ib * 8 * 128 * SEG + th * 1024,
                            [[SEG, 128], [128 * SEG, 8], [1, 1024]]),
                    ("hhbuf", "b_hh2"), ("b_hh2",))
            for dt_ in range(NDT):
                dp = dt_ % 2
                dwt, dwk = dwb[dp], f"b_dwb{dp}"
                DMA("gpsimd", dwt[:],
                    bass.AP(dr["dwP"].tensor, dt_ * NIT * 128 * 128,
                            [[128, 128], [16384, NIT], [1, 128]]),
                    (dwk,), (dwk,))
                DMA("scalar", x2s[dp][:],
                    x2buf[dt_ * 128:(dt_ + 1) * 128,
                          th * 1024:(th + 1) * 1024],
                    ("x2buf", f"b_x2s{dp}"), (f"b_x2s{dp}",))
                for nn in range(2):
                    bi = nn + 2 * dp
                    bk = f"psb{bi}"
                    for it in range(NIT):
                        MM(psb[bi][0:128, 0:512],
                           dwt[:, it * 128:(it + 1) * 128],
                           hh2[:, it * 1024 + nn * 512:
                               it * 1024 + nn * 512 + 512],
                           (dwk, "b_hh2"), bk,
                           start=(it == 0), stop=False)
                    MM(psb[bi][0:128, 0:512], c128row[:],
                       shh2[0:1, th * 1024 + nn * 512:
                            th * 1024 + nn * 512 + 512],
                       ("c_c128row", "b_shh2"), bk, start=False,
                       stop=False)
                    MM(psb[bi][0:128, 0:512], c128row[:],
                       shl2[0:1, th * 1024 + nn * 512:
                            th * 1024 + nn * 512 + 512],
                       ("c_c128row", "b_shl2"), bk, start=False, stop=True)
                    STT(yt[nn][:], psb[bi][0:128, 0:512],
                        dsc[:, dt_:dt_ + 1],
                        x2s[dp][:, nn * 512:(nn + 1) * 512],
                        ALU.mult, ALU.add,
                        (bk, "b_dsc", f"b_x2s{dp}"), (f"b_yt{nn}",))
                    DMA("sync",
                        outT[dt_ * 128:(dt_ + 1) * 128,
                             th * 1024 + nn * 512:
                             th * 1024 + nn * 512 + 512],
                        yt[nn][:], (f"b_yt{nn}", "outT"), ("outT",))
        ar5.free()

        E("gpsimd", lambda gp: None, reads=("outT",))

        pr.finalize(nc, gctx)
        nc.compile()
    return nc


CONSTS = host_consts()
_NC = None


def _get_nc():
    global _NC
    if _NC is None:
        _NC = build_core_program()
    return _NC


def kernel(**inputs):
    bf = ml_dtypes.bfloat16
    f32 = np.float32
    x = np.asarray(inputs["L0_attn_hidden_states"], f32)
    Wq = np.asarray(inputs["L0_attn_Wq"], f32)
    Wk = np.asarray(inputs["L0_attn_Wk"], f32)
    Wv = np.asarray(inputs["L0_attn_Wv"], f32)
    conv_q = np.asarray(inputs["L0_attn_conv_q"], f32)
    conv_k = np.asarray(inputs["L0_attn_conv_k"], f32)
    conv_v = np.asarray(inputs["L0_attn_conv_v"], f32)
    Wf1 = np.asarray(inputs["L0_attn_Wf1"], f32)
    Wf2 = np.asarray(inputs["L0_attn_Wf2"], f32)
    Wb = np.asarray(inputs["L0_attn_Wb"], f32)
    A_log = np.asarray(inputs["L0_attn_A_log"], f32)
    dt_bias = np.asarray(inputs["L0_attn_dt_bias"], f32)
    Wg = np.asarray(inputs["L0_attn_Wg"], f32)
    bg = np.asarray(inputs["L0_attn_bg"], f32)
    o_norm_w = np.asarray(inputs["L0_attn_o_norm_w"], f32)
    Wo = np.asarray(inputs["L0_attn_Wo"], f32)
    gate_w = np.asarray(inputs["L0_ffn_gate_w"])
    gate_scale = np.asarray(inputs["L0_ffn_gate_scale"], f32)
    up_w = np.asarray(inputs["L0_ffn_up_w"])
    up_scale = np.asarray(inputs["L0_ffn_up_scale"], f32)
    down_w = np.asarray(inputs["L0_ffn_down_w"])
    down_scale = np.asarray(inputs["L0_ffn_down_scale"], f32)

    def cc(a):
        return np.ascontiguousarray(a)

    shared = dict(
        WqP=cc(Wq.reshape(D, H, 128).transpose(1, 0, 2)),
        WkP=cc(Wk.reshape(D, H, 128).transpose(1, 0, 2)),
        WvP=cc(Wv.reshape(D, H, 128).transpose(1, 0, 2)),
        WgP=cc(Wg.reshape(D, H, 128).transpose(1, 0, 2)),
        Wf1=cc(Wf1),
        Wf2P=cc(Wf2.reshape(128, H, 128).transpose(1, 0, 2)),
        WbP=cc(Wb),
        WoP=cc(Wo.reshape(H, 128, D)),
        convP=cc(np.concatenate([conv_q.reshape(H, 128, 4),
                                 conv_k.reshape(H, 128, 4),
                                 conv_v.reshape(H, 128, 4)], axis=2)),
        dtbP=cc(dt_bias.reshape(H, 128).T),
        AlogP=cc(A_log.reshape(16, 1)),
        onwP=cc(o_norm_w.reshape(1, 128)),
        bgP=cc(bg.reshape(H, 128)),
        gwP=cc(gate_w.T.reshape(D, NIT, 128).transpose(1, 0, 2)
               .astype(bf)),
        uwP=cc(up_w.T.reshape(D, NIT, 128).transpose(1, 0, 2).astype(bf)),
        dwP=cc(down_w.reshape(16, 128, NIT, 128).transpose(0, 2, 3, 1)
               .astype(bf)),
        gsP=cc(gate_scale.reshape(NIT, 128)),
        usP=cc(up_scale.reshape(NIT, 128)),
        dsP=cc(down_scale.reshape(16, 128).T),
        **CONSTS)

    in_maps = []
    for c in range(8):
        b_, s_ = c // 4, c % 4
        xseg = cc(x[b_, s_ * SEG:(s_ + 1) * SEG, :].T)
        if s_ > 0:
            xh = cc(x[b_, s_ * SEG - 3:s_ * SEG, :].T)
        else:
            xh = np.zeros((D, 3), f32)
        gm = np.zeros((128, 8), f32)
        for j in range(8):
            if j // 4 == b_ and j % 4 < s_:
                gm[:, j] = 1.0
        in_maps.append(dict(xT=xseg, xhT=xh, gam=gm, **shared))

    nc = _get_nc()
    res = run_bass_kernel_spmd(nc, in_maps, core_ids=list(range(8)))
    out = np.empty((B, T, D), f32)
    for c in range(8):
        b_, s_ = c // 4, c % 4
        out[b_, s_ * SEG:(s_ + 1) * SEG, :] = \
            np.asarray(res.results[c]["outT"], f32).T
    return out


# version 08
# speedup vs baseline: 1.0411x; 1.0411x over previous
import contextlib
import numpy as np
import ml_dtypes
import concourse.bacc as bacc
import concourse.bass as bass
import concourse.mybir as mybir
from concourse.bass_utils import run_bass_kernel_spmd

F32 = mybir.dt.float32
BF16 = mybir.dt.bfloat16
ALU = mybir.AluOpType
ACTF = mybir.ActivationFunctionType
AX = mybir.AxisListType

B, T, D, H = 2, 8192, 2048, 16
SEG, C = 2048, 128
NCH = SEG // C
DVP = 256
LEVELS = (8, 16, 32, 64)
NT4 = SEG // 512
NDT = D // 128
NIT = 8192 // 128
NSL = ((0, 512), (512, 1024), (1024, 1536), (1536, 2048), (2048, 2051))

ENGINES = ("sync", "scalar", "tensor", "vector", "gpsimd")
DMA_SLOTS = {"sync": 10, "scalar": 6, "tensor": 1, "vector": 1, "gpsimd": 8}

CFVALS = (1e-5, 128.0 * 1e-6, 1e-6, 128.0, -1.0, 1.0 / 128.0)


class _Op:
    __slots__ = ("eng", "fn", "reads", "writes", "dma", "val_after",
                 "waits", "slot", "slot_use")

    def __init__(self, eng, fn, reads, writes, dma):
        self.eng, self.fn = eng, fn
        self.reads, self.writes, self.dma = reads, writes, dma
        self.val_after = 0
        self.waits = []
        self.slot = None
        self.slot_use = 0


class Prog:

    def __init__(self):
        self.ops = {e: [] for e in ENGINES}
        self.last_write = {}
        self.readers = {}
        self.needed = set()
        self.barrier_ops = []
        self.dmas_since_barrier = []
        self.n_ops = 0

    def emit(self, eng, fn, reads=(), writes=(), dma=False):
        op = _Op(eng, fn, tuple(reads), tuple(writes), dma)
        self.ops[eng].append(op)
        self.n_ops += 1
        if dma:
            self.dmas_since_barrier.append(op)
        deps = list(self.barrier_ops)
        for b_ in op.reads:
            w = self.last_write.get(b_)
            if w is not None:
                deps.append(w)
        for b_ in op.writes:
            w = self.last_write.get(b_)
            if w is not None:
                deps.append(w)
            deps.extend(self.readers.get(b_, ()))
        seen = set()
        for d in deps:
            if d is op or id(d) in seen:
                continue
            seen.add(id(d))
            op.waits.append(d)
            self.needed.add(d)
        for b_ in op.reads:
            self.readers.setdefault(b_, []).append(op)
        for b_ in op.writes:
            self.last_write[b_] = op
            self.readers[b_] = []
        return op

    def barrier(self):
        last = [self.ops[e][-1] for e in ENGINES if self.ops[e]]
        self.barrier_ops = last + self.dmas_since_barrier
        for d in last:
            self.needed.add(d)
        self.dmas_since_barrier = []
        self.last_write = {}
        self.readers = {}

    def finalize(self, nc, ctx):
        slot_uses = {}
        for e in ENGINES:
            rot = 0
            for op in self.ops[e]:
                if op.dma:
                    s = (e, rot % DMA_SLOTS[e])
                    rot += 1
                    op.slot = s
                    op.slot_use = slot_uses.get(s, 0)
                    slot_uses[s] = op.slot_use + 1
        for e in ENGINES:
            for op in reversed(self.ops[e]):
                if not op.dma:
                    self.needed.add(op)
                    break
        for e in ENGINES:
            v = 0
            for op in self.ops[e]:
                if not op.dma and op in self.needed:
                    v += 1
                op.val_after = v
        sems = {e: ctx.enter_context(nc.semaphore(f"sem_{e}"))
                for e in ENGINES if any(not o.dma for o in self.ops[e])}
        slot_sems = {s: ctx.enter_context(nc.semaphore(f"dma_{s[0]}_{s[1]}"))
                     for s in slot_uses}
        block = ctx.enter_context(nc.Block())

        def run_engine(e):
            def body(engine):
                cur = {e2: 0 for e2 in ENGINES}
                cur_slot = {s: 0 for s in slot_sems}
                for op in self.ops[e]:
                    for d in op.waits:
                        if d.dma:
                            tgt = 16 * (d.slot_use + 1)
                            if cur_slot[d.slot] < tgt:
                                engine.wait_ge(slot_sems[d.slot], tgt)
                                cur_slot[d.slot] = tgt
                        elif cur[d.eng] < d.val_after:
                            engine.wait_ge(sems[d.eng], d.val_after)
                            cur[d.eng] = d.val_after
                    if op.dma:
                        tgt = 16 * op.slot_use
                        if tgt > 0 and cur_slot[op.slot] < tgt:
                            engine.wait_ge(slot_sems[op.slot], tgt)
                            cur_slot[op.slot] = tgt
                        op.fn(engine).then_inc(slot_sems[op.slot], 16)
                    else:
                        inst = op.fn(engine)
                        if op in self.needed:
                            if inst is None:
                                inst = engine.nop()
                            inst.then_inc(sems[e], 1)
            return body

        for e in ENGINES:
            if self.ops[e]:
                getattr(block, e)(run_engine(e))


def host_consts():
    bf = ml_dtypes.bfloat16
    ident = np.eye(128, dtype=bf)
    bandmask = np.zeros((128, 18), dtype=bf)
    for s in range(128):
        for d in range(1, 8):
            if (s % 8) + d <= 7:
                bandmask[s, d] = 1.0
                bandmask[s, 8 + d] = 1.0
        bandmask[s, 8] = 1.0
        bandmask[s, 16] = 1.0
        bandmask[s, 17] = 1.0
    ohbig = np.zeros((128, 18 * 18), dtype=bf)
    for j in range(1, 8):
        ohbig[:, j * 18 + j] = -1.0
    for d in range(0, 8):
        ohbig[:, (8 + d) * 18 + 8 + d] = 1.0
    ohbig[0, 16 * 18 + 16] = -1.0
    shA = np.zeros((128, 128), dtype=bf)
    shB = np.zeros((128, 128), dtype=bf)
    for j in range(4):
        for i in range(8):
            m = 32 * j + i
            (shA if j % 2 == 0 else shB)[m, m + 16] = 1.0
    onescol = np.ones((128, 2), dtype=bf)
    onesrow = np.ones((1, 128), dtype=bf)
    c128row = np.full((1, 128), -128.0, dtype=bf)
    hsel = np.zeros((16, 16 * 128), dtype=bf)
    for h in range(16):
        hsel[h, h * 128:(h + 1) * 128] = 1.0
    cflP = np.tile(np.array([CFVALS], dtype=np.float32), (128, 1))
    identf = np.eye(16, dtype=np.float32)
    return dict(ident=ident, bandmask=bandmask, ohbig=ohbig, shA=shA,
                shB=shB, onescol=onescol, onesrow=onesrow, c128row=c128row,
                hsel=hsel, cflP=cflP, identf=identf)


def build_core_program():
    nc = bacc.Bacc("TRN2", debug=False)
    dr = {}

    def din(name, shape, dt):
        dr[name] = nc.dram_tensor(name, shape, dt, kind="ExternalInput").ap()

    din("xT", [D, SEG], F32)
    din("xhT", [D, 3], F32)
    din("gam", [128, 8], F32)
    din("WqP", [H, D, 128], F32)
    din("WkP", [H, D, 128], F32)
    din("WvP", [H, D, 128], F32)
    din("WgP", [H, D, 128], F32)
    din("Wf1", [D, 128], F32)
    din("Wf2P", [H, 128, 128], F32)
    din("WbP", [D, 16], F32)
    din("WoP", [H, 128, D], F32)
    din("convP", [H, 128, 12], F32)
    din("dtbP", [128, H], F32)
    din("AlogP", [16, 1], F32)
    din("onwP", [1, 128], F32)
    din("bgP", [H, 128], F32)
    din("gwP", [NIT, D, 128], BF16)
    din("uwP", [NIT, D, 128], BF16)
    din("dwP", [16, NIT, 128, 128], BF16)
    din("gsP", [NIT, 128], F32)
    din("usP", [NIT, 128], F32)
    din("dsP", [128, 16], F32)
    for nm, shp, dt in (("ident", [128, 128], BF16),
                        ("bandmask", [128, 18], BF16),
                        ("ohbig", [128, 324], BF16),
                        ("shA", [128, 128], BF16), ("shB", [128, 128], BF16),
                        ("onescol", [128, 2], BF16),
                        ("onesrow", [1, 128], BF16),
                        ("c128row", [1, 128], BF16),
                        ("hsel", [16, 2048], BF16),
                        ("cflP", [128, len(CFVALS)], F32),
                        ("identf", [16, 16], F32)):
        din(nm, shp, dt)
    outT = nc.dram_tensor("outT", [D, SEG], F32, kind="ExternalOutput").ap()
    shear = nc.dram_tensor("shearbuf", [2, NCH, 16512], BF16).ap()
    fq = nc.dram_tensor("fqbuf", [H, 128, SEG], BF16).ap()
    fk = nc.dram_tensor("fkbuf", [H, 128, SEG], BF16).ap()
    fv = nc.dram_tensor("fvbuf", [H, 128, SEG], BF16).ap()
    fg = nc.dram_tensor("fgbuf", [H, 128, SEG], F32).ap()
    fbb = nc.dram_tensor("fbbuf", [H, 128, SEG], BF16).ap()
    obuf = nc.dram_tensor("obuf", [H, 128, SEG], BF16).ap()
    rbuf = nc.dram_tensor("respbuf", [H, 128, SEG], BF16).ap()
    sgate = nc.dram_tensor("sgatebuf", [H, 128, SEG], BF16).ap()
    x2buf = nc.dram_tensor("x2buf", [D, SEG], F32).ap()
    hhbuf = nc.dram_tensor("hhbuf", [NIT, 128, SEG], BF16).ap()
    shbuf = nc.dram_tensor("shbuf", [2, SEG], BF16).ap()
    agin = nc.dram_tensor("aginbuf", [H * 128, DVP], BF16).ap()
    agout = nc.dram_tensor("agoutbuf", [8 * H * 128, DVP], BF16).ap()

    pr = Prog()
    E = pr.emit
    gctx = contextlib.ExitStack()
    with gctx:
        def gsb(name, shape, dt=F32):
            return gctx.enter_context(nc.sbuf_tensor(name, shape, dt))

        ident = gsb("c_ident", [128, 128], BF16)
        bmask = gsb("c_bandmask", [128, 18], BF16)
        ohbig = gsb("c_ohbig", [128, 324], BF16)
        shmA = gsb("c_shA", [128, 128], BF16)
        shmB = gsb("c_shB", [128, 128], BF16)
        onescol = gsb("c_onescol", [128, 2], BF16)
        onesrow = gsb("c_onesrow", [1, 128], BF16)
        onesr5 = gsb("c_onesr5", [1, 512], BF16)
        c128row = gsb("c_c128row", [1, 128], BF16)
        hsel = gsb("c_hsel", [16, 2048], BF16)
        gam = gsb("c_gam", [128, 8], F32)
        onw = gsb("c_onw", [128, 128], BF16)
        aposP = gsb("c_aposP", [128, 16], F32)
        dtb = gsb("c_dtb", [128, H], F32)
        dtbn = gsb("c_dtbn", [128, H], F32)
        scr16 = gsb("c_scr16", [16, 8], F32)
        cfl = gsb("c_cfl", [128, len(CFVALS)], F32)
        for ci, cv in enumerate(CFVALS):
            nc.const_aps.register(F32, cv, cfl[0:128, ci:ci + 1])
        identf = gsb("c_identf", [16, 16], F32)
        aprow = gsb("c_aprow", [1, 16], F32)

        psb = {i: gctx.enter_context(
            nc.psum_tensor(f"ps_b{i}", [128, 512], F32)) for i in range(7)}
        pstp = gctx.enter_context(nc.psum_tensor("ps_tp", [128, 1024], BF16))

        def MM(out_ap, lhsT_ap, rhs_ap, reads, bank, start, stop, tp=None):
            E("tensor", (lambda o, l, r, s1, s2, tpp: lambda t: t.matmul(
                o, l, r, start=s1, stop=s2, tile_position=tpp))(
                out_ap, lhsT_ap, rhs_ap, start, stop, tp),
              reads=tuple(reads) + (bank,), writes=(bank,))

        def TPo(out_ap, in_ap, id_ap, reads, bank):
            E("tensor", (lambda o, i, d: lambda t: t.transpose(o, i, d))(
                out_ap, in_ap, id_ap),
              reads=tuple(reads) + ("c_ident", bank), writes=(bank,))

        def VCP(out_ap, in_ap, reads, writes):
            E("vector", (lambda o, i: lambda v: v.tensor_copy(o, i))(
                out_ap, in_ap), reads=reads, writes=writes)

        def ACP(out_ap, in_ap, reads, writes):
            E("scalar", (lambda o, i: lambda s: s.copy(o, i))(out_ap, in_ap),
              reads=reads, writes=writes)

        def TT(out_ap, a_ap, b_ap, op, reads, writes):
            E("vector", (lambda o, a, b_, p: lambda v: v.tensor_tensor(
                o, a, b_, p))(out_ap, a_ap, b_ap, op),
              reads=reads, writes=writes)

        def TSm(out_ap, a_ap, scalar, reads, writes):
            E("vector", (lambda o, a, c: lambda v: v.tensor_scalar_mul(
                o, a, c))(out_ap, a_ap, scalar), reads=reads, writes=writes)

        def STT(out_ap, in0_ap, scalar, in1_ap, op0, op1, reads, writes):
            E("vector", (lambda o, a, s, b_, p0, p1: lambda v:
                         v.scalar_tensor_tensor(o, a, s, b_, p0, p1))(
                out_ap, in0_ap, scalar, in1_ap, op0, op1),
              reads=reads, writes=writes)

        def ACT(out_ap, in_ap, func, reads, writes, bias=0.0, scale=1.0):
            extra = ()
            if (isinstance(bias, float) and bias != 0.0) or \
                    (isinstance(scale, float) and scale != 1.0):
                extra = ("c_cfl",)
            E("scalar", (lambda o, i, f, bb, sc: lambda s: s.activation(
                o, i, f, bias=bb, scale=sc))(out_ap, in_ap, func, bias,
                                             scale),
              reads=tuple(reads) + extra, writes=writes)

        def DMA(eng, out_ap, in_ap, reads, writes):
            E(eng, (lambda o, i: lambda s: s.dma_start(out=o, in_=i))(
                out_ap, in_ap), reads=reads, writes=writes, dma=True)

        def MS(ap, val, writes, reads=()):
            E("vector", (lambda a, vv: lambda v: v.memset(a, vv))(ap, val),
              reads=reads, writes=writes)

        def RCP(ap, reads, writes):
            E("vector", (lambda a: lambda v: v.reciprocal(a, a))(ap),
              reads=reads, writes=writes)

        def dsl(t_ap, i, p_, n_):
            return bass.AP(t_ap.tensor, i * p_ * n_, [[n_, p_], [1, n_]])

        for nm, dst in (("ident", ident), ("bandmask", bmask),
                        ("ohbig", ohbig), ("shA", shmA), ("shB", shmB),
                        ("onescol", onescol), ("onesrow", onesrow),
                        ("c128row", c128row), ("hsel", hsel), ("gam", gam)):
            DMA("sync", dst[:], dr[nm], (), (f"c_{nm}",))
        MS(onesr5[:], 1.0, ("c_onesr5",))
        DMA("sync", scr16[0:16, 0:1], dr["AlogP"], (), ("alog",))
        DMA("sync", cfl[:], dr["cflP"], (), ("c_cfl",))
        DMA("sync", dtb[:], dr["dtbP"], (), ("c_dtb",))
        TSm(dtbn[:], dtb[:], -1.0, ("c_dtb",), ("c_dtbn",))
        ACT(scr16[0:16, 1:2], scr16[0:16, 0:1], ACTF.Exp, ("alog",),
            ("alog",))
        DMA("sync", identf[:], dr["identf"], (), ("c_identf",))
        E("tensor", lambda t: t.transpose(
            psb[6][0:1, 16:32], scr16[0:16, 1:2], identf[0:16, 0:16]),
          reads=("alog", "c_identf", "psb6"), writes=("psb6",))
        VCP(aprow[0:1, 0:16], psb[6][0:1, 16:32], ("psb6",), ("c_aprow",))
        E("gpsimd", lambda gp: gp.partition_broadcast(
            aposP[:], aprow[0:1, 0:16]),
          reads=("c_aprow",), writes=("c_aposP",))
        DMA("gpsimd", onw[0:1, :], dr["onwP"], ("c_onw",), ("onwrow",))
        MM(psb[6][0:128, 128:256], onesrow[:], onw[0:1, :],
           ("onwrow", "c_onesrow"), "psb6", start=True, stop=True)
        VCP(onw[:], psb[6][0:128, 128:256], ("psb6",), ("c_onw",))

        def emit_layernorm(tag, t, src_ap, src_key, xt, xkey, sq, sqkey):
            ks = {k_: f"{tag}_{k_}" for k_ in t}
            for dt_ in range(NDT):
                DMA("sync", t["xs"][:], src_ap[dt_ * 128:(dt_ + 1) * 128, :],
                    (src_key, ks["xs"]), (ks["xs"],))
                ACP(xt[dt_][:, 3:SEG + 3], t["xs"][:], (ks["xs"],),
                    (xkey(dt_),))
                ACT(sq[:, 0:SEG], t["xs"][:], ACTF.Square, (ks["xs"],),
                    (sqkey,))
                for n in range(NT4):
                    MM(psb[n][0:1, 0:512], onescol[:, 0:1],
                       xt[dt_][:, 3 + n * 512:3 + (n + 1) * 512],
                       (xkey(dt_), "c_onescol"), f"psb{n}",
                       start=(dt_ == 0), stop=(dt_ == NDT - 1))
                    MM(psb[n][32:33, 0:512], onescol[:, 0:1],
                       sq[:, n * 512:(n + 1) * 512],
                       (sqkey, "c_onescol"), f"psb{n}",
                       start=(dt_ == 0), stop=(dt_ == NDT - 1))
            for n in range(NT4):
                VCP(t["smrow"][0:1, n * 512:(n + 1) * 512],
                    psb[n][0:1, 0:512], (f"psb{n}",), (ks["smrow"],))
                VCP(t["sqrow"][0:1, n * 512:(n + 1) * 512],
                    psb[n][32:33, 0:512], (f"psb{n}",), (ks["sqrow"],))
            TSm(t["smrow"][:], t["smrow"][:], 1.0 / D, (ks["smrow"],),
                (ks["smrow"],))
            ACP(t["mrb"][:], t["smrow"][:], (ks["smrow"],), (ks["mrb"],))
            TSm(t["sqrow"][:], t["sqrow"][:], 1.0 / D, (ks["sqrow"],),
                (ks["sqrow"],))
            TT(t["smrow"][:], t["smrow"][:], t["smrow"][:], ALU.mult,
               (ks["smrow"],), (ks["smrow"],))
            TT(t["sqrow"][:], t["sqrow"][:], t["smrow"][:], ALU.subtract,
               (ks["sqrow"], ks["smrow"]), (ks["sqrow"],))
            ACT(t["sqrow"][:], t["sqrow"][:], ACTF.Sqrt,
                (ks["sqrow"],), (ks["sqrow"],), bias=1e-5)
            RCP(t["sqrow"][:], (ks["sqrow"],), (ks["sqrow"],))
            ACP(t["ivb"][:], t["sqrow"][:], (ks["sqrow"],), (ks["ivb"],))
            for n in range(NT4):
                MM(psb[4][0:128, 0:512], onesrow[:],
                   t["mrb"][0:1, n * 512:(n + 1) * 512],
                   (ks["mrb"], "c_onesrow"), "psb4", start=True, stop=True)
                VCP(t["murepb"][:, n * 512:(n + 1) * 512],
                    psb[4][0:128, 0:512], ("psb4",), (ks["murepb"],))
                MM(psb[5][0:128, 0:512], onesrow[:],
                   t["ivb"][0:1, n * 512:(n + 1) * 512],
                   (ks["ivb"], "c_onesrow"), "psb5", start=True, stop=True)
                VCP(t["ivrepb"][:, n * 512:(n + 1) * 512],
                    psb[5][0:128, 0:512], ("psb5",), (ks["ivrepb"],))
            for dt_ in range(NDT):
                TT(xt[dt_][:, 3:SEG + 3], xt[dt_][:, 3:SEG + 3],
                   t["murepb"][:], ALU.subtract,
                   (xkey(dt_), ks["murepb"]), (xkey(dt_),))
                TT(xt[dt_][:, 3:SEG + 3], xt[dt_][:, 3:SEG + 3],
                   t["ivrepb"][:], ALU.mult,
                   (xkey(dt_), ks["ivrepb"]), (xkey(dt_),))

        ar1 = bass.AutoArena(nc)
        xln = {i: ar1.sbuf_tensor(f"p_xln{i}", [128, SEG + 3], BF16)
               for i in range(NDT)}
        xh = {i: ar1.sbuf_tensor(f"p_xh{i}", [128, 3], F32)
              for i in range(NDT)}
        xh3sq = ar1.sbuf_tensor("p_xh3sq", [128, 4], BF16)
        hrow = ar1.sbuf_tensor("p_hrow", [1, 16], F32)
        hrowb = ar1.sbuf_tensor("p_hrowb", [1, 16], BF16)
        ln1 = {"xs": ar1.sbuf_tensor("p_xs", [128, SEG], F32),
               "smrow": ar1.sbuf_tensor("p_smrow", [1, SEG], F32),
               "sqrow": ar1.sbuf_tensor("p_sqrow", [1, SEG], F32),
               "mrb": ar1.sbuf_tensor("p_mrb", [1, SEG], BF16),
               "ivb": ar1.sbuf_tensor("p_ivb", [1, SEG], BF16),
               "murepb": ar1.sbuf_tensor("p_murepb", [128, SEG], BF16),
               "ivrepb": ar1.sbuf_tensor("p_ivrepb", [128, SEG], BF16)}
        wbuf = {i: ar1.sbuf_tensor(f"p_wbuf{i}", [128, 2048], BF16)
                for i in range(2)}
        wsmall = ar1.sbuf_tensor("p_wsmall", [128, 256], BF16)
        zq = ar1.sbuf_tensor("p_zq", [128, SEG + 3], BF16)
        zk = ar1.sbuf_tensor("p_zk", [128, SEG + 3], BF16)
        zv = ar1.sbuf_tensor("p_zv", [128, SEG + 3], BF16)
        cq = ar1.sbuf_tensor("p_cq", [128, SEG], BF16)
        ck = ar1.sbuf_tensor("p_ck", [128, SEG], BF16)
        cvv = ar1.sbuf_tensor("p_cvv", [128, SEG], BF16)
        sqf = ar1.sbuf_tensor("p_sqf", [128, 512], BF16)
        ssrow = ar1.sbuf_tensor("p_ssrow", [1, SEG], F32)
        ssb = ar1.sbuf_tensor("p_ssb", [1, SEG], BF16)
        ssrep = ar1.sbuf_tensor("p_ssrep", [128, 512], BF16)
        zf1 = ar1.sbuf_tensor("p_zf1", [128, SEG], BF16)
        bsig = ar1.sbuf_tensor("p_bsig", [16, SEG], BF16)
        brep = ar1.sbuf_tensor("p_brep", [128, SEG], BF16)
        gsg = ar1.sbuf_tensor("p_gsg", [128, 512], F32)
        gfl = ar1.sbuf_tensor("p_gfl", [128, 512], F32)
        convh = ar1.sbuf_tensor("p_convh", [128, 12], F32)
        wf2h = ar1.sbuf_tensor("p_wf2h", [128, 128], BF16)
        bgh = ar1.sbuf_tensor("p_bgh", [1, 128], BF16)
        gts = ar1.sbuf_tensor("p_gts", [128, SEG], BF16)
        ar1.allocate()

        emit_layernorm("p", ln1, dr["xT"], "xT", xln, lambda d: f"xln{d}",
                       zq, "zq")
        for dt_ in range(NDT):
            DMA("sync", xh[dt_][:], dr["xhT"][dt_ * 128:(dt_ + 1) * 128, :],
                (), (f"xh{dt_}",))
        for dt_ in range(NDT):
            ACP(xln[dt_][:, 0:3], xh[dt_][:], (f"xh{dt_}",), (f"xln{dt_}",))
            ACT(xh3sq[:, 0:3], xh[dt_][:], ACTF.Square, (f"xh{dt_}",),
                ("xh3sq",))
            MM(psb[6][0:1, 0:3], onescol[:, 0:1], xln[dt_][:, 0:3],
               (f"xln{dt_}", "c_onescol"), "psb6",
               start=(dt_ == 0), stop=(dt_ == NDT - 1))
            MM(psb[6][32:33, 0:3], onescol[:, 0:1], xh3sq[:, 0:3],
               ("xh3sq", "c_onescol"), "psb6",
               start=(dt_ == 0), stop=(dt_ == NDT - 1))
        VCP(hrow[0:1, 0:3], psb[6][0:1, 0:3], ("psb6",), ("hrow",))
        VCP(hrow[0:1, 8:11], psb[6][32:33, 0:3], ("psb6",), ("hrow",))
        TSm(hrow[0:1, 0:3], hrow[0:1, 0:3], 1.0 / D, ("hrow",), ("hrow",))
        ACP(hrowb[0:1, 0:3], hrow[0:1, 0:3], ("hrow",), ("hrowb",))
        TSm(hrow[0:1, 8:11], hrow[0:1, 8:11], 1.0 / D, ("hrow",), ("hrow",))
        TT(hrow[0:1, 4:7], hrow[0:1, 0:3], hrow[0:1, 0:3], ALU.mult,
           ("hrow",), ("hrow",))
        TT(hrow[0:1, 8:11], hrow[0:1, 8:11], hrow[0:1, 4:7], ALU.subtract,
           ("hrow",), ("hrow",))
        ACT(hrow[0:1, 8:11], hrow[0:1, 8:11], ACTF.Sqrt, ("hrow",),
            ("hrow",), bias=1e-5)
        RCP(hrow[0:1, 8:11], ("hrow",), ("hrow",))
        ACP(hrowb[0:1, 8:11], hrow[0:1, 8:11], ("hrow",), ("hrowb",))
        MM(psb[6][0:128, 0:3], onesrow[:], hrowb[0:1, 0:3],
           ("hrowb", "c_onesrow"), "psb6", start=True, stop=True)
        VCP(ln1["murepb"][:, 0:3], psb[6][0:128, 0:3], ("psb6",),
            ("p_murepb",))
        MM(psb[6][0:128, 4:7], onesrow[:], hrowb[0:1, 8:11],
           ("hrowb", "c_onesrow"), "psb6", start=True, stop=True)
        VCP(ln1["ivrepb"][:, 0:3], psb[6][0:128, 4:7], ("psb6",),
            ("p_ivrepb",))
        for dt_ in range(NDT):
            TT(xln[dt_][:, 0:3], xln[dt_][:, 0:3], ln1["murepb"][:, 0:3],
               ALU.subtract, (f"xln{dt_}", "p_murepb"), (f"xln{dt_}",))
            TT(xln[dt_][:, 0:3], xln[dt_][:, 0:3], ln1["ivrepb"][:, 0:3],
               ALU.mult, (f"xln{dt_}", "p_ivrepb"), (f"xln{dt_}",))

        DMA("gpsimd", wsmall[:],
            bass.AP(dr["WbP"].tensor, 0, [[16, 128], [2048, 16], [1, 16]]),
            (), ("wb",))
        for n in range(NT4):
            for dt_ in range(NDT):
                MM(psb[n][0:16, 0:512], wsmall[:, dt_ * 16:dt_ * 16 + 16],
                   xln[dt_][:, 3 + n * 512:3 + (n + 1) * 512],
                   (f"xln{dt_}", "wb"), f"psb{n}",
                   start=(dt_ == 0), stop=(dt_ == NDT - 1))
        for n in range(NT4):
            ACT(bsig[0:16, n * 512:(n + 1) * 512], psb[n][0:16, 0:512],
                ACTF.Sigmoid, (f"psb{n}",), ("bsig",))
        DMA("gpsimd", wbuf[1][:],
            bass.AP(dr["Wf1"].tensor, 0,
                    [[128, 128], [128 * 128, 16], [1, 128]]),
            ("pw1",), ("pw1",))
        for n in range(NT4):
            for dt_ in range(NDT):
                MM(psb[4][0:128, 0:512],
                   wbuf[1][:, dt_ * 128:(dt_ + 1) * 128],
                   xln[dt_][:, 3 + n * 512:3 + (n + 1) * 512],
                   (f"xln{dt_}", "pw1"), "psb4",
                   start=(dt_ == 0), stop=(dt_ == NDT - 1))
            ACP(zf1[:, n * 512:(n + 1) * 512], psb[4][0:128, 0:512],
                ("psb4",), ("zf1",))

        for h in range(H):
            ws = wbuf[h % 2]
            wk_ = f"pw{h % 2}"
            for mi, (wnm, ztile, zkey) in enumerate(
                    (("WqP", zq, "zq"), ("WkP", zk, "zk"),
                     ("WvP", zv, "zv"))):
                DMA("gpsimd", ws[:],
                    bass.AP(dr[wnm].tensor, h * D * 128,
                            [[128, 128], [128 * 128, 16], [1, 128]]),
                    (wk_,), (wk_,))
                for (a, b_) in NSL:
                    for dt_ in range(NDT):
                        MM(psb[mi][0:128, 0:b_ - a],
                           ws[:, dt_ * 128:(dt_ + 1) * 128],
                           xln[dt_][:, a:b_],
                           (f"xln{dt_}", wk_), f"psb{mi}",
                           start=(dt_ == 0), stop=(dt_ == NDT - 1))
                    ACP(ztile[:, a:b_], psb[mi][0:128, 0:b_ - a],
                        (f"psb{mi}",), (zkey,))
            DMA("sync", convh[:], dsl(dr["convP"], h, 128, 12),
                ("convh",), ("convh",))
            for (ztile, zkey, co, cokey, c0) in (
                    (zq, "zq", cq, "cq", 0), (zk, "zk", ck, "ck", 4),
                    (zv, "zv", cvv, "cvv", 8)):
                MS(co[:], 0.0, (cokey,))
                for j in (2, 3, 1, 0):
                    STT(co[:], ztile[:, j:SEG + j],
                        convh[:, c0 + j:c0 + j + 1], co[:],
                        ALU.mult, ALU.add,
                        ("convh", zkey, cokey), (cokey,))
            ACT(cq[:], cq[:], ACTF.Silu, ("cq",), ("cq",))
            ACT(ck[:], ck[:], ACTF.Silu, ("ck",), ("ck",))
            ACT(cvv[:], cvv[:], ACTF.Silu, ("cvv",), ("cvv",))
            for n in range(NT4):
                MM(psb[3][0:128, 0:512], hsel[:, h * 128:(h + 1) * 128],
                   bsig[0:16, n * 512:(n + 1) * 512],
                   ("bsig", "c_hsel"), "psb3", start=True, stop=True)
                ACP(brep[:, n * 512:(n + 1) * 512], psb[3][0:128, 0:512],
                    ("psb3",), ("brep",))
            TT(cvv[:], cvv[:], brep[:], ALU.mult, ("cvv", "brep"), ("cvv",))
            DMA("scalar", dsl(fv, h, 128, SEG), cvv[:], ("cvv",), ("w_fv",))
            DMA("scalar", dsl(fbb, h, 128, SEG), brep[:], ("brep",),
                ("w_fbb",))
            for (src, skey, dstb, fold) in ((cq, "cq", fq, True),
                                            (ck, "ck", fk, False)):
                for n in range(NT4):
                    ACT(sqf[:], src[:, n * 512:(n + 1) * 512], ACTF.Square,
                        (skey,), ("sqf",))
                    MM(psb[4][0:1, 0:512], onescol[:, 0:1], sqf[:],
                       ("sqf", "c_onescol"), "psb4", start=True, stop=True)
                    VCP(ssrow[0:1, n * 512:(n + 1) * 512],
                        psb[4][0:1, 0:512], ("psb4",), ("ssrow",))
                ACT(ssrow[:], ssrow[:], ACTF.Sqrt, ("ssrow",), ("ssrow",),
                    bias=(128.0 * 1e-6 if fold else 1e-6),
                    scale=(128.0 if fold else 1.0))
                RCP(ssrow[:], ("ssrow",), ("ssrow",))
                ACP(ssb[:], ssrow[:], ("ssrow",), ("ssb",))
                for n in range(NT4):
                    MM(psb[4][0:128, 0:512], onesrow[:],
                       ssb[0:1, n * 512:(n + 1) * 512],
                       ("ssb", "c_onesrow"), "psb4", start=True, stop=True)
                    VCP(ssrep[:], psb[4][0:128, 0:512], ("psb4",),
                        ("ssrep",))
                    TT(src[:, n * 512:(n + 1) * 512],
                       src[:, n * 512:(n + 1) * 512], ssrep[:], ALU.mult,
                       (skey, "ssrep"), (skey,))
                DMA("scalar", dsl(dstb, h, 128, SEG), src[:], (skey,),
                    ("w_fql",))
            DMA("gpsimd", wf2h[:], dsl(dr["Wf2P"], h, 128, 128),
                ("wf2h",), ("wf2h",))
            for n in range(NT4):
                MM(psb[5][0:128, 0:512], wf2h[:],
                   zf1[:, n * 512:(n + 1) * 512],
                   ("zf1", "wf2h"), "psb5", start=True, stop=True)
                E("scalar", (lambda h_: lambda s: s.activation(
                    gsg[:], psb[5][0:128, 0:512], ACTF.Sigmoid,
                    bias=dtbn[:, h_:h_ + 1], scale=-1.0))(h),
                  reads=("psb5", "c_dtbn", "c_cfl"), writes=("gsg",))
                ACT(gsg[:], gsg[:], ACTF.Ln, ("gsg",), ("gsg",))
                E("scalar", (lambda h_: lambda s: s.activation(
                    gfl[:], gsg[:], ACTF.Exp,
                    scale=aposP[:, h_:h_ + 1]))(h),
                  reads=("gsg", "c_aposP"), writes=("gfl",))
                DMA("scalar",
                    bass.AP(fg.tensor, h * 128 * SEG + n * 512,
                            [[SEG, 128], [1, 512]]),
                    gfl[:], ("gfl",), ("w_fg",))
            DMA("gpsimd", ws[:],
                bass.AP(dr["WgP"].tensor, h * D * 128,
                        [[128, 128], [128 * 128, 16], [1, 128]]),
                (wk_,), (wk_,))
            DMA("gpsimd", bgh[:],
                bass.AP(dr["bgP"].tensor, h * 128, [[128, 1], [1, 128]]),
                ("bgh",), ("bgh",))
            for n in range(NT4):
                for dt_ in range(NDT):
                    MM(psb[6][0:128, 0:512],
                       ws[:, dt_ * 128:(dt_ + 1) * 128],
                       xln[dt_][:, 3 + n * 512:3 + (n + 1) * 512],
                       (f"xln{dt_}", wk_), "psb6",
                       start=(dt_ == 0), stop=False)
                MM(psb[6][0:128, 0:512], bgh[0:1, 0:128], onesr5[:],
                   ("bgh", "c_onesr5"), "psb6", start=False, stop=True)
                ACT(gts[:, n * 512:(n + 1) * 512], psb[6][0:128, 0:512],
                    ACTF.Silu, ("psb6",), ("gts",))
            DMA("scalar", dsl(sgate, h, 128, SEG), gts[:], ("gts",),
                ("w_sg",))
        ar1.free()

        pr.barrier()

        ar2 = bass.AutoArena(nc)
        q = ar2.sbuf_tensor("s_q", [128, SEG], BF16)
        k = ar2.sbuf_tensor("s_k", [128, SEG], BF16)
        vb = ar2.sbuf_tensor("s_vbt", [128, SEG], BF16)
        g = ar2.sbuf_tensor("s_g", [128, SEG], F32)
        brs = ar2.sbuf_tensor("s_brs", [128, SEG], BF16)
        kb = ar2.sbuf_tensor("s_kb", [128, SEG], BF16)
        Dp = ar2.sbuf_tensor("s_Dp", [128, SEG], F32)
        Dq = ar2.sbuf_tensor("s_Dq", [128, SEG], F32)
        Rp = ar2.sbuf_tensor("s_Rp", [128, SEG], F32)
        Rq = ar2.sbuf_tensor("s_Rq", [128, SEG], F32)
        ETp = ar2.sbuf_tensor("s_ETp", [128, SEG], BF16)
        ETq = ar2.sbuf_tensor("s_ETq", [128, SEG], BF16)
        EAp = ar2.sbuf_tensor("s_EAp", [128, SEG], BF16)
        EAq = ar2.sbuf_tensor("s_EAq", [128, SEG], BF16)
        FTp = ar2.sbuf_tensor("s_FTp", [128, SEG], BF16)
        FTq = ar2.sbuf_tensor("s_FTq", [128, SEG], BF16)
        FAp = ar2.sbuf_tensor("s_FAp", [128, SEG], BF16)
        FAq = ar2.sbuf_tensor("s_FAq", [128, SEG], BF16)
        kbD = {l: ar2.sbuf_tensor(f"s_kbD{l}", [128, SEG], BF16)
               for l in LEVELS}
        qD = {l: ar2.sbuf_tensor(f"s_qD{l}", [128, SEG], BF16)
              for l in LEVELS}
        kR = {l: ar2.sbuf_tensor(f"s_kR{l}", [128, SEG], BF16)
              for l in LEVELS}
        Ktil = ar2.sbuf_tensor("s_Ktil", [128, SEG], BF16)
        Qtil = ar2.sbuf_tensor("s_Qtil", [128, SEG], BF16)
        Khat = ar2.sbuf_tensor("s_Khat", [128, SEG], BF16)
        GammaC = ar2.sbuf_tensor("s_GammaC", [128, NCH], F32)
        dcm = ar2.sbuf_tensor("s_dcm", [32, SEG], BF16)
        dtm = ar2.sbuf_tensor("s_dtm", [128, NCH * 18], BF16)
        btm = ar2.sbuf_tensor("s_btm", [128, NCH], F32)
        bandTF = ar2.sbuf_tensor("s_bandTF", [128, SEG], BF16)
        bandAF = ar2.sbuf_tensor("s_bandAF", [128, SEG], BF16)
        Amat = {c_: ar2.sbuf_tensor(f"s_Amat{c_}", [128, 128], BF16)
                for c_ in range(NCH)}
        mats = []
        for p_ in range(2):
            mats.append(dict(
                ck=f"s_comp{p_}",
                Pt=ar2.sbuf_tensor(f"s_Pt{p_}", [128, 128], BF16),
                Pn=ar2.sbuf_tensor(f"s_Pn{p_}", [128, 128], BF16),
                comp=ar2.sbuf_tensor(f"s_comp{p_}", [128, 128], BF16),
                Qpow={j: ar2.sbuf_tensor(f"s_Qpow{p_}_{j}", [128, 128],
                                         BF16) for j in range(1, 6)},
                Ppw={j: ar2.sbuf_tensor(f"s_Ppw{p_}_{j}", [128, 128],
                                        BF16) for j in range(1, 6)},
                IQ={j: ar2.sbuf_tensor(f"s_IQ{p_}_{j}", [128, 128],
                                       BF16) for j in range(0, 7)}))
        S = ar2.sbuf_tensor("s_S", [128, DVP], F32)
        Sbf = ar2.sbuf_tensor("s_Sbf", [128, DVP], BF16)
        Ktm = ar2.sbuf_tensor("s_Ktm", [128, 128], BF16)
        vtm = ar2.sbuf_tensor("s_vtm", [128, 128], BF16)
        rr = ar2.sbuf_tensor("s_rr", [128, DVP], BF16)
        ww = ar2.sbuf_tensor("s_ww", [128, DVP], BF16)
        oloc = ar2.sbuf_tensor("s_oloc", [128, 512], BF16)
        rcml = ar2.sbuf_tensor("s_rcml", [128, 512], BF16)
        sfinb = ar2.sbuf_tensor("s_sfinb", [128, DVP], BF16)
        ar2.allocate()

        def blk(t_, lvl, half):
            r = t_.rearrange("p (n b) -> p n b", b=2 * lvl)
            return r[:, :, 0:lvl] if half == "lo" else r[:, :, lvl:2 * lvl]

        def bcol(t_, lvl, col, cnt):
            r = t_.rearrange("p (n b) -> p n b", b=2 * lvl)
            return r[:, :, col:col + 1].broadcast_to(
                (128, SEG // (2 * lvl), cnt))

        MS(kbD[8][:, 0:1032], 0.0, ("s_kbD8",))
        for r_ in range(4):
            DMA("sync", bass.AP(shear.tensor, r_ * 132096,
                                [[1032, 128], [1, 1032]]),
                kbD[8][:, 0:1032], ("s_kbD8",), ("shearbuf",))
        for t_, k_ in ((FTp, "s_FTd"), (FTq, "s_FTd"), (FAp, "s_FAd"),
                       (FAq, "s_FAd"), (ETq, "s_ETq"), (EAq, "s_EAq")):
            MS(t_[:], 0.0, (k_,))

        band_banks = ["psb0", "psb1", "psb2", "psb3"]

        def band_mms(Ftile, Fkey, slot, first, last=False):
            for n in range(NT4):
                MM(psb[n][0:18, 0:512], ohbig[:, slot * 18:(slot + 1) * 18],
                   Ftile[:, n * 512:(n + 1) * 512],
                   (Fkey, "c_ohbig"), band_banks[n], start=first, stop=last)

        def emit_nodes(cc, bank_t, bank_key, rhsF, rhsP, M):
            base = cc * C
            MS(psb[6][0:128, 0:128], 0.0, ("psb6",))
            for j in (1, 3, 5, 7):
                crow = 16 * j - 16
                tp = (0, 96) if crow == 96 else None
                MM(psb[6][crow:crow + 8, 16 * j + 8:16 * j + 16],
                   kR[8][:, base + 16 * j:base + 16 * j + 8],
                   rhsF[8][:, base + 16 * j + 8:base + 16 * j + 16],
                   ("s_kR8", f"{rhsP}8"), "psb6", start=True, stop=True,
                   tp=tp)
            ACP(M["comp"][:], psb[6][0:128, 0:128], ("psb6",), (M["ck"],))
            MM(bank_t[0:128, 0:128], shmA[:], M["comp"][:],
               ("c_shA", M["ck"]), bank_key, start=True, stop=False)

            def nmm(rlo, clo, m_, n_, lvl_):
                tp = (0, 96) if rlo == 96 else None
                MM(bank_t[rlo:rlo + m_, clo:clo + n_],
                   kR[lvl_][:, base + rlo:base + rlo + m_],
                   rhsF[lvl_][:, base + clo:base + clo + n_],
                   (f"s_kR{lvl_}", f"{rhsP}{lvl_}"), bank_key,
                   start=False, stop=False, tp=tp)

            nmm(0, 64, 64, 64, 64)
            nmm(0, 32, 32, 32, 32)
            nmm(64, 96, 32, 32, 32)
            for i in range(4):
                nmm(32 * i, 32 * i + 16, 16, 16, 16)
            for j in (0, 2, 4, 6):
                nmm(16 * j, 16 * j + 8, 8, 8, 8)
            MM(bank_t[0:128, 0:128], shmB[:], M["comp"][:],
               ("c_shB", M["ck"]), bank_key, start=False, stop=True)

        for h in range(H):
            DMA("sync", q[:], dsl(fq, h, 128, SEG), ("w_fql", "s_q"),
                ("s_q",))
            DMA("sync", k[:], dsl(fk, h, 128, SEG), ("w_fql", "s_k"),
                ("s_k",))
            DMA("sync", vb[:], dsl(fv, h, 128, SEG), ("w_fv", "s_vbt"),
                ("s_vbt",))
            DMA("sync", g[:], dsl(fg, h, 128, SEG), ("w_fg", "s_g"),
                ("s_g",))
            DMA("sync", brs[:], dsl(fbb, h, 128, SEG), ("w_fbb", "s_brs"),
                ("s_brs",))
            TT(kb[:], k[:], brs[:], ALU.mult, ("s_k", "s_brs"), ("s_kb",))
            MS(S[:, 0:128], 0.0, ("s_S",))
            VCP(S[:, 128:256], ident[:], ("c_ident",), ("s_S",))

            VCP(blk(Dp, 1, "lo"), blk(g, 1, "lo"), ("s_g",), ("s_Dp",))
            TT(blk(Dp, 1, "hi"), blk(g, 1, "hi"), blk(g, 1, "lo"),
               ALU.mult, ("s_g",), ("s_Dp",))
            MS(Rp[:], 1.0, ("s_Rp",))
            VCP(blk(Rp, 1, "lo"), blk(g, 1, "hi"), ("s_g",), ("s_Rp",))
            Dc, Dn_, Dk_, Dnk = Dp, Dq, "s_Dp", "s_Dq"
            Rc, Rn_, Rk_, Rnk = Rp, Rq, "s_Rp", "s_Rq"
            lvl = 2
            while True:
                if lvl in LEVELS:
                    TT(kbD[lvl][:], kb[:], Dc[:], ALU.mult,
                       ("s_kb", Dk_), (f"s_kbD{lvl}",))
                    TT(qD[lvl][:], q[:], Dc[:], ALU.mult,
                       ("s_q", Dk_), (f"s_qD{lvl}",))
                    TT(kR[lvl][:], k[:], Rc[:], ALU.mult,
                       ("s_k", Rk_), (f"s_kR{lvl}",))
                if lvl == 128:
                    VCP(GammaC[:], Dc.rearrange(
                        "p (n b) -> p n b", b=C)[:, :, C - 1:C].squeeze(2),
                        (Dk_,), ("s_GammaC",))
                    TT(Ktil[:], k[:], Dc[:], ALU.mult, ("s_k", Dk_),
                       ("s_Ktil",))
                    TT(Qtil[:], q[:], Dc[:], ALU.mult, ("s_q", Dk_),
                       ("s_Qtil",))
                    TT(Khat[:], k[:], Rc[:], ALU.mult, ("s_k", Rk_),
                       ("s_Khat",))
                    break
                l = lvl
                VCP(blk(Rn_, l, "hi"), blk(Rc, l, "hi"), (Rk_,), (Rnk,))
                TT(blk(Rn_, l, "lo"), blk(Rc, l, "lo"),
                   bcol(Dc, l, 2 * l - 1, l), ALU.mult, (Rk_, Dk_), (Rnk,))
                Rc, Rn_, Rk_, Rnk = Rn_, Rc, Rnk, Rk_
                VCP(blk(Dn_, l, "lo"), blk(Dc, l, "lo"), (Dk_,), (Dnk,))
                TT(blk(Dn_, l, "hi"), blk(Dc, l, "hi"),
                   bcol(Dc, l, l - 1, l), ALU.mult, (Dk_,), (Dnk,))
                Dc, Dn_, Dk_, Dnk = Dn_, Dc, Dnk, Dk_
                lvl *= 2

            TT(ETp[:], kb[:], g[:], ALU.mult, ("s_kb", "s_g"), ("s_ETp",))
            TT(EAp[:], q[:], g[:], ALU.mult, ("s_q", "s_g"), ("s_EAp",))
            TT(FAp[:], q[:], k[:], ALU.mult, ("s_q", "s_k"), ("s_FAd",))
            band_mms(FAp, "s_FAd", 8, True)
            ET, ETn, ETk, ETnk = ETp, ETq, "s_ETp", "s_ETq"
            EA, EAn, EAk, EAnk = EAp, EAq, "s_EAp", "s_EAq"
            FTc, FTn = FTp, FTq
            FAc, FAn = FAq, FAp
            for d in range(1, 8):
                TT(FTc[:, 0:SEG - d], ET[:, d:SEG], k[:, 0:SEG - d],
                   ALU.mult, (ETk, "s_k"), ("s_FTd",))
                band_mms(FTc, "s_FTd", d, False)
                TT(FAc[:, 0:SEG - d], EA[:, d:SEG], k[:, 0:SEG - d],
                   ALU.mult, (EAk, "s_k"), ("s_FAd",))
                band_mms(FAc, "s_FAd", 8 + d, False, last=(d == 7))
                FTc, FTn = FTn, FTc
                FAc, FAn = FAn, FAc
                if d < 7:
                    TT(ETn[:, d:SEG], ET[:, d:SEG], g[:, 0:SEG - d],
                       ALU.mult, (ETk, "s_g"), (ETnk,))
                    TT(EAn[:, d:SEG], EA[:, d:SEG], g[:, 0:SEG - d],
                       ALU.mult, (EAk, "s_g"), (EAnk,))
                    ET, ETn, ETk, ETnk = ETn, ET, ETnk, ETk
                    EA, EAn, EAk, EAnk = EAn, EA, EAnk, EAk
            for n in range(NT4):
                VCP(dcm[0:18, n * 512:(n + 1) * 512], psb[n][0:18, 0:512],
                    (band_banks[n],), ("s_dcm",))
            DMA("gpsimd", dcm[16:17, :], brs[0:1, :], ("s_brs", "s_dcm"),
                ("s_dcm",))
            for cc in range(NCH):
                TPo(pstp[0:128, 0:18], dcm[0:18, cc * 128:(cc + 1) * 128],
                    ident[0:18, 0:18], ("s_dcm",), "pstpW0")
                VCP(dtm[:, cc * 18:(cc + 1) * 18], pstp[0:128, 0:18],
                    ("pstpW0",), ("s_dtm",))
                TSm(btm[:, cc:cc + 1], dtm[:, cc * 18 + 16:cc * 18 + 17],
                    -1.0, ("s_dtm",), ("s_btm",))
            TT(dtm.rearrange("p (c e) -> p c e", e=18)[:, :, 0:16],
               dtm.rearrange("p (c e) -> p c e", e=18)[:, :, 0:16],
               bmask[:, 0:16].unsqueeze(1).broadcast_to((128, NCH, 16)),
               ALU.mult, ("s_dtm", "c_bandmask"), ("s_dtm",))
            for m_ in range(2):
                DMA("sync", bass.AP(shear.tensor, m_ * 264192,
                                    [[129, 128], [16512, NCH], [1, 8]]),
                    bass.AP(dtm, m_ * 8, [[NCH * 18, 128], [18, NCH],
                                          [1, 8]]),
                    ("s_dtm", "shearbuf"), ("shearbuf",))
            DMA("sync", bass.AP(bandTF, 0, [[SEG, 128], [128, NCH],
                                            [1, 128]]),
                bass.AP(shear.tensor, 0, [[128, 128], [16512, NCH],
                                          [1, 128]]),
                ("shearbuf", "s_bandTF"), ("s_bandTF",))
            DMA("sync", bass.AP(bandAF, 0, [[SEG, 128], [128, NCH],
                                            [1, 128]]),
                bass.AP(shear.tensor, 264192, [[128, 128], [16512, NCH],
                                               [1, 128]]),
                ("shearbuf", "s_bandAF"), ("s_bandAF",))

            for cc in range(NCH):
                base = cc * C
                p_ = cc % 2
                M = mats[p_]
                bT, bTk = psb[2 * p_], f"psb{2 * p_}"
                bA, bAk = psb[2 * p_ + 1], f"psb{2 * p_ + 1}"
                emit_nodes(cc, bT, bTk, kbD, "s_kbD", M)
                STT(M["Pt"][:], bT[0:128, 0:128], -1.0,
                    bandTF[:, base:base + C], ALU.mult, ALU.add,
                    (bTk, "s_bandTF"), (f"s_Pt{p_}",))
                emit_nodes(cc, bA, bAk, qD, "s_qD", M)
                STT(Amat[cc][:], bA[0:128, 0:128], 1.0,
                    bandAF[:, base:base + C], ALU.mult, ALU.add,
                    (bAk, "s_bandAF"), (f"s_Amat{cc}",))
                tw = "pstpW1" if p_ == 0 else "pstpW3"
                twc = 256 if p_ == 0 else 768
                TPo(pstp[0:128, twc:twc + 128], M["Pt"][:], ident[:],
                    (f"s_Pt{p_}",), tw)
                ACP(M["Pn"][:], pstp[0:128, twc:twc + 128], (tw,),
                    (f"s_Pn{p_}",))
                TT(M["IQ"][0][:], M["Pt"][:], ident[:], ALU.add,
                   (f"s_Pt{p_}", "c_ident"), (f"s_IQ{p_}_0",))
                prevQ, prevQk = M["Pt"], f"s_Pt{p_}"
                prevP, prevPk = M["Pn"], f"s_Pn{p_}"
                for j in range(1, 7):
                    MM(psb[6][0:128, 0:128], prevP[:], prevQ[:],
                       (prevPk, prevQk), "psb6", start=True, stop=True)
                    if j < 6:
                        ACP(M["Qpow"][j][:], psb[6][0:128, 0:128],
                            ("psb6",), (f"s_Qpow{p_}_{j}",))
                        TT(M["IQ"][j][:], M["Qpow"][j][:], ident[:],
                           ALU.add, (f"s_Qpow{p_}_{j}", "c_ident"),
                           (f"s_IQ{p_}_{j}",))
                        TPo(pstp[0:128, twc:twc + 128], M["Qpow"][j][:],
                            ident[:], (f"s_Qpow{p_}_{j}",), tw)
                        ACP(M["Ppw"][j][:], pstp[0:128, twc:twc + 128],
                            (tw,), (f"s_Ppw{p_}_{j}",))
                        prevQ, prevQk = M["Qpow"][j], f"s_Qpow{p_}_{j}"
                        prevP, prevPk = M["Ppw"][j], f"s_Ppw{p_}_{j}"
                    else:
                        TT(M["IQ"][j][:], psb[6][0:128, 0:128], ident[:],
                           ALU.add, ("psb6", "c_ident"), (f"s_IQ{p_}_{j}",))
                TPo(pstp[0:128, 512:640], Khat[:, base:base + C], ident[:],
                    ("s_Khat",), "pstpW2")
                ACP(Ktm[:], pstp[0:128, 512:640], ("pstpW2",), ("s_Ktm",))
                TPo(pstp[0:128, 640:768], vb[:, base:base + C], ident[:],
                    ("s_vbt",), "pstpW2")
                ACP(vtm[:], pstp[0:128, 640:768], ("pstpW2",), ("s_vtm",))
                ACP(Sbf[:], S[:], ("s_S",), ("s_Sbf",))
                MM(psb[4][0:128, 0:DVP], Ktil[:, base:base + C], Sbf[:],
                   ("s_Ktil", "s_Sbf"), "psb4", start=True, stop=True)
                STT(rr[:, 0:128], psb[4][0:128, 0:128], btm[:, cc:cc + 1],
                    vtm[:], ALU.mult, ALU.add,
                    ("psb4", "s_btm", "s_vtm"), ("s_rr",))
                E("vector", (lambda cc_: lambda v: v.tensor_scalar_mul(
                    rr[:, 128:256], psb[4][0:128, 128:256],
                    btm[:, cc_:cc_ + 1]))(cc),
                  reads=("psb4", "s_btm"), writes=("s_rr",))
                wcur, wk2 = rr, "s_rr"
                for j in (6, 5, 4, 3, 2, 1, 0):
                    MM(psb[4][0:128, 0:DVP], M["IQ"][j][:], wcur[:],
                       (f"s_IQ{p_}_{j}", wk2), "psb4", start=True,
                       stop=True)
                    dst, dk2 = (ww, "s_ww") if wcur is rr else (rr, "s_rr")
                    ACP(dst[:], psb[4][0:128, 0:DVP], ("psb4",), (dk2,))
                    wcur, wk2 = dst, dk2
                MM(psb[5][0:128, 0:128], Qtil[:, base:base + C],
                   Sbf[:, 0:128], ("s_Qtil", "s_Sbf"), "psb5",
                   start=True, stop=False)
                MM(psb[5][0:128, 0:128], Amat[cc][:], wcur[:, 0:128],
                   (f"s_Amat{cc}", wk2), "psb5", start=False, stop=True)
                MM(psb[5][0:128, 128:256], Sbf[:, 128:256],
                   Qtil[:, base:base + C], ("s_Qtil", "s_Sbf"), "psb5",
                   start=True, stop=False)
                MM(psb[5][0:128, 128:256], wcur[:, 128:256], Amat[cc][:],
                   (f"s_Amat{cc}", wk2), "psb5", start=False, stop=True)
                q4 = cc % 4
                VCP(oloc[:, q4 * 128:(q4 + 1) * 128], psb[5][0:128, 0:128],
                    ("psb5",), ("s_oloc",))
                VCP(rcml[:, q4 * 128:(q4 + 1) * 128],
                    psb[5][0:128, 128:256], ("psb5",), ("s_rcml",))
                if q4 == 3:
                    c0 = cc - 3
                    DMA("scalar", bass.AP(obuf.tensor,
                                          h * 128 * SEG + c0 * 128,
                                          [[SEG, 128], [128, 4], [1, 128]]),
                        oloc[:], ("s_oloc", "obuf"), ("obuf",))
                    DMA("scalar", bass.AP(rbuf.tensor,
                                          h * 128 * SEG + c0 * 128,
                                          [[SEG, 128], [128, 4], [1, 128]]),
                        rcml[:], ("s_rcml", "respbuf"), ("respbuf",))
                MM(psb[4][0:128, 256:512], Ktm[:], wcur[:],
                   ("s_Ktm", wk2), "psb4", start=True, stop=True)
                STT(S[:], S[:], GammaC[:, cc:cc + 1],
                    psb[4][0:128, 256:512], ALU.mult, ALU.add,
                    ("s_S", "s_GammaC", "psb4"), ("s_S",))
            ACP(sfinb[:], S[:], ("s_S",), ("s_sfinb",))
            TPo(pstp[0:128, 0:128], sfinb[:, 128:256], ident[:],
                ("s_sfinb",), "pstpW0")
            ACP(sfinb[:, 128:256], pstp[0:128, 0:128], ("pstpW0",),
                ("s_sfinb",))
            DMA("sync", agin[h * 128:(h + 1) * 128, :], sfinb[:],
                ("s_sfinb", "aginbuf"), ("aginbuf",))

        E("gpsimd", lambda gp: gp.collective_compute(
            "AllGather", ALU.bypass, replica_groups=[list(range(8))],
            ins=[agin.opt()], outs=[agout.opt()]),
          reads=("aginbuf",), writes=("agoutbuf",))
        ar2.free()

        pr.barrier()

        ar3 = bass.AutoArena(nc)
        eph = []
        for p_ in range(2):
            eph.append(dict(
                agS=ar3.sbuf_tensor(f"e_agS{p_}", [128, 8 * DVP], BF16),
                Sacc=ar3.sbuf_tensor(f"e_Sacc{p_}", [128, 128], F32),
                Saccb=ar3.sbuf_tensor(f"e_Saccb{p_}", [128, 128], BF16),
                t1e=ar3.sbuf_tensor(f"e_t1e{p_}", [128, 128], F32),
                oLh=ar3.sbuf_tensor(f"e_oLh{p_}", [128, SEG], BF16),
                rCh=ar3.sbuf_tensor(f"e_rCh{p_}", [128, SEG], BF16),
                gth=ar3.sbuf_tensor(f"e_gth{p_}", [128, SEG], BF16),
                ofin=ar3.sbuf_tensor(f"e_ofin{p_}", [128, 128], F32),
                osq=ar3.sbuf_tensor(f"e_osq{p_}", [128, 128], F32),
                onrm=ar3.sbuf_tensor(f"e_onrm{p_}", [128, 1], F32),
                ont=ar3.sbuf_tensor(f"e_ont{p_}", [128, 128], BF16)))
        onc = {hh: ar3.sbuf_tensor(f"e_onc{hh}", [128, SEG], BF16)
               for hh in range(H)}
        wo2 = {i: ar3.sbuf_tensor(f"e_wo2{i}", [128, 2048], BF16)
               for i in range(2)}
        x2t = ar3.sbuf_tensor("e_x2t", [128, SEG], F32)
        xres = ar3.sbuf_tensor("e_xres", [128, SEG], F32)
        ar3.allocate()

        for h in range(H):
            p_ = h % 2
            ep = eph[p_]
            kk = {nm: f"e_{nm}{p_}" for nm in
                  ("agS", "Sacc", "Saccb", "t1e", "oLh", "rCh", "gth",
                   "ofin", "osq", "onrm", "ont")}
            DMA("sync", ep["agS"][:],
                bass.AP(agout.tensor, h * 128 * DVP,
                        [[DVP, 128], [H * 128 * DVP, 8], [1, DVP]]),
                ("agoutbuf", kk["agS"]), (kk["agS"],))
            MS(ep["Sacc"][:], 0.0, (kk["Sacc"],))
            for j in range(8):
                ACP(ep["Saccb"][:], ep["Sacc"][:], (kk["Sacc"],),
                    (kk["Saccb"],))
                MM(psb[p_][0:128, 0:128],
                   ep["agS"][:, j * DVP + 128:(j + 1) * DVP],
                   ep["Saccb"][:], (kk["agS"], kk["Saccb"]), f"psb{p_}",
                   start=True, stop=True)
                TT(ep["t1e"][:], psb[p_][0:128, 0:128],
                   ep["agS"][:, j * DVP:j * DVP + 128], ALU.add,
                   (f"psb{p_}", kk["agS"]), (kk["t1e"],))
                TT(ep["t1e"][:], ep["t1e"][:], ep["Sacc"][:], ALU.subtract,
                   (kk["t1e"], kk["Sacc"]), (kk["t1e"],))
                STT(ep["Sacc"][:], ep["t1e"][:], gam[:, j:j + 1],
                    ep["Sacc"][:], ALU.mult, ALU.add,
                    (kk["t1e"], "c_gam", kk["Sacc"]), (kk["Sacc"],))
            ACP(ep["Saccb"][:], ep["Sacc"][:], (kk["Sacc"],),
                (kk["Saccb"],))
            DMA("sync", ep["oLh"][:], dsl(obuf, h, 128, SEG),
                ("obuf", kk["oLh"]), (kk["oLh"],))
            DMA("sync", ep["rCh"][:], dsl(rbuf, h, 128, SEG),
                ("respbuf", kk["rCh"]), (kk["rCh"],))
            DMA("sync", ep["gth"][:], dsl(sgate, h, 128, SEG),
                ("w_sg", kk["gth"]), (kk["gth"],))
            for cc in range(NCH):
                MM(psb[2 + p_][0:128, 0:128],
                   ep["rCh"][:, cc * 128:(cc + 1) * 128], ep["Saccb"][:],
                   (kk["rCh"], kk["Saccb"]), f"psb{2 + p_}",
                   start=True, stop=True)
                TT(ep["ofin"][:], psb[2 + p_][0:128, 0:128],
                   ep["oLh"][:, cc * 128:(cc + 1) * 128], ALU.add,
                   (f"psb{2 + p_}", kk["oLh"]), (kk["ofin"],))
                ACT(ep["osq"][:], ep["ofin"][:], ACTF.Square,
                    (kk["ofin"],), (kk["osq"],))
                E("vector", (lambda ep_: lambda v: v.tensor_reduce(
                    ep_["onrm"][:],
                    ep_["osq"].rearrange("p (a b) -> p a b", b=128),
                    AX.X, ALU.add))(ep),
                  reads=(kk["osq"],), writes=(kk["onrm"],))
                ACT(ep["onrm"][:], ep["onrm"][:], ACTF.Sqrt,
                    (kk["onrm"],), (kk["onrm"],), bias=1e-5,
                    scale=1.0 / 128.0)
                RCP(ep["onrm"][:], (kk["onrm"],), (kk["onrm"],))
                STT(ep["ont"][:], ep["ofin"][:], ep["onrm"][:, 0:1],
                    onw[:], ALU.mult, ALU.mult,
                    (kk["ofin"], kk["onrm"], "c_onw"), (kk["ont"],))
                tw = "pstpW1" if p_ == 0 else "pstpW3"
                twc = 256 if p_ == 0 else 768
                TPo(pstp[0:128, twc:twc + 128], ep["ont"][:], ident[:],
                    (kk["ont"],), tw)
                TT(onc[h][:, cc * 128:(cc + 1) * 128],
                   pstp[0:128, twc:twc + 128],
                   ep["gth"][:, cc * 128:(cc + 1) * 128], ALU.mult,
                   (tw, kk["gth"]), (f"e_onc{h}",))
        for dt_ in range(NDT):
            wo = wo2[dt_ % 2]
            wok = f"e_wo2{dt_ % 2}"
            DMA("gpsimd", wo[:],
                bass.AP(dr["WoP"].tensor, dt_ * 128,
                        [[D, 128], [128 * D, 16], [1, 128]]),
                (wok,), (wok,))
            for n in range(NT4):
                bi = 4 + n % 3
                bk = f"psb{bi}"
                for h in range(H):
                    MM(psb[bi][0:128, 0:512], wo[:, h * 128:(h + 1) * 128],
                       onc[h][:, n * 512:(n + 1) * 512],
                       (f"e_onc{h}", wok), bk,
                       start=(h == 0), stop=(h == H - 1))
                VCP(x2t[:, n * 512:(n + 1) * 512], psb[bi][0:128, 0:512],
                    (bk,), ("e_x2t",))
            DMA("sync", xres[:], dr["xT"][dt_ * 128:(dt_ + 1) * 128, :],
                ("e_xres",), ("e_xres",))
            TT(x2t[:], x2t[:], xres[:], ALU.add, ("e_x2t", "e_xres"),
               ("e_x2t",))
            DMA("scalar", x2buf[dt_ * 128:(dt_ + 1) * 128, :], x2t[:],
                ("e_x2t", "x2buf"), ("x2buf",))
        ar3.free()

        pr.barrier()

        ar4 = bass.AutoArena(nc)
        xf = {i: ar4.sbuf_tensor(f"f_xf{i}", [128, SEG + 3], BF16)
              for i in range(NDT)}
        ln2 = {"xs": ar4.sbuf_tensor("f_xs", [128, SEG], F32),
               "smrow": ar4.sbuf_tensor("f_smrow", [1, SEG], F32),
               "sqrow": ar4.sbuf_tensor("f_sqrow", [1, SEG], F32),
               "mrb": ar4.sbuf_tensor("f_mrb", [1, SEG], BF16),
               "ivb": ar4.sbuf_tensor("f_ivb", [1, SEG], BF16),
               "murepb": ar4.sbuf_tensor("f_murepb", [128, SEG], BF16),
               "ivrepb": ar4.sbuf_tensor("f_ivrepb", [128, SEG], BF16)}
        sqs = ar4.sbuf_tensor("f_sqs", [128, SEG], BF16)
        gw = {i: ar4.sbuf_tensor(f"f_gw{i}", [128, 2048], BF16)
              for i in range(2)}
        uw = {i: ar4.sbuf_tensor(f"f_uw{i}", [128, 2048], BF16)
              for i in range(2)}
        gsc = {i: ar4.sbuf_tensor(f"f_gsc{i}", [128, 1], F32)
               for i in range(2)}
        usc = {i: ar4.sbuf_tensor(f"f_usc{i}", [128, 1], F32)
               for i in range(2)}
        gsil = ar4.sbuf_tensor("f_gsil", [128, SEG], BF16)
        uup = ar4.sbuf_tensor("f_uup", [128, SEG], BF16)
        hht = ar4.sbuf_tensor("f_hht", [128, SEG], BF16)
        sxrow = ar4.sbuf_tensor("f_sxrow", [1, SEG], F32)
        sxh = ar4.sbuf_tensor("f_sxh", [1, SEG], BF16)
        sxl = ar4.sbuf_tensor("f_sxl", [1, SEG], BF16)
        shr = ar4.sbuf_tensor("f_shr", [1, SEG], F32)
        shh = ar4.sbuf_tensor("f_shh", [1, SEG], BF16)
        shl = ar4.sbuf_tensor("f_shl", [1, SEG], BF16)
        ar4.allocate()

        emit_layernorm("f", ln2, x2buf, "x2buf", xf, lambda d: f"xf{d}",
                       sqs, "f_sqs")
        for n in range(NT4):
            for dt_ in range(NDT):
                MM(psb[n][0:1, 0:512], onescol[:, 0:1],
                   xf[dt_][:, 3 + n * 512:3 + (n + 1) * 512],
                   (f"xf{dt_}", "c_onescol"), f"psb{n}",
                   start=(dt_ == 0), stop=(dt_ == NDT - 1))
            VCP(sxrow[0:1, n * 512:(n + 1) * 512], psb[n][0:1, 0:512],
                (f"psb{n}",), ("f_sxrow",))
        ACP(sxh[:], sxrow[:], ("f_sxrow",), ("f_sxh",))
        TT(sxrow[:], sxrow[:], sxh[:], ALU.subtract, ("f_sxrow", "f_sxh"),
           ("f_sxrow",))
        ACP(sxl[:], sxrow[:], ("f_sxrow",), ("f_sxl",))
        for it in range(NIT):
            sl = it % 2
            DMA("sync", gw[sl][:],
                bass.AP(dr["gwP"].tensor, it * D * 128,
                        [[128, 128], [128 * 128, 16], [1, 128]]),
                (f"f_gw{sl}",), (f"f_gw{sl}",))
            DMA("gpsimd", uw[sl][:],
                bass.AP(dr["uwP"].tensor, it * D * 128,
                        [[128, 128], [128 * 128, 16], [1, 128]]),
                (f"f_uw{sl}",), (f"f_uw{sl}",))
            DMA("gpsimd", gsc[sl][:],
                bass.AP(dr["gsP"].tensor, it * 128, [[1, 128], [1, 1]]),
                (f"f_gsc{sl}",), (f"f_gsc{sl}",))
            DMA("gpsimd", usc[sl][:],
                bass.AP(dr["usP"].tensor, it * 128, [[1, 128], [1, 1]]),
                (f"f_usc{sl}",), (f"f_usc{sl}",))
            for n in range(NT4):
                for dt_ in range(NDT):
                    MM(psb[0][0:128, 0:512],
                       gw[sl][:, dt_ * 128:(dt_ + 1) * 128],
                       xf[dt_][:, 3 + n * 512:3 + (n + 1) * 512],
                       (f"xf{dt_}", f"f_gw{sl}"), "psb0",
                       start=(dt_ == 0), stop=False)
                MM(psb[0][0:128, 0:512], c128row[:],
                   sxh[0:1, n * 512:(n + 1) * 512],
                   ("c_c128row", "f_sxh"), "psb0", start=False, stop=False)
                MM(psb[0][0:128, 0:512], c128row[:],
                   sxl[0:1, n * 512:(n + 1) * 512],
                   ("c_c128row", "f_sxl"), "psb0", start=False, stop=True)
                E("scalar", (lambda n_, sl_: lambda s: s.activation(
                    gsil[:, n_ * 512:(n_ + 1) * 512], psb[0][0:128, 0:512],
                    ACTF.Silu, scale=gsc[sl_][:, 0:1]))(n, sl),
                  reads=("psb0", f"f_gsc{sl}"), writes=("f_gsil",))
                for dt_ in range(NDT):
                    MM(psb[1][0:128, 0:512],
                       uw[sl][:, dt_ * 128:(dt_ + 1) * 128],
                       xf[dt_][:, 3 + n * 512:3 + (n + 1) * 512],
                       (f"xf{dt_}", f"f_uw{sl}"), "psb1",
                       start=(dt_ == 0), stop=False)
                MM(psb[1][0:128, 0:512], c128row[:],
                   sxh[0:1, n * 512:(n + 1) * 512],
                   ("c_c128row", "f_sxh"), "psb1", start=False, stop=False)
                MM(psb[1][0:128, 0:512], c128row[:],
                   sxl[0:1, n * 512:(n + 1) * 512],
                   ("c_c128row", "f_sxl"), "psb1", start=False, stop=True)
                E("scalar", (lambda n_, sl_: lambda s: s.activation(
                    uup[:, n_ * 512:(n_ + 1) * 512], psb[1][0:128, 0:512],
                    ACTF.Copy, scale=usc[sl_][:, 0:1]))(n, sl),
                  reads=("psb1", f"f_usc{sl}"), writes=("f_uup",))
            TT(hht[:], gsil[:], uup[:], ALU.mult, ("f_gsil", "f_uup"),
               ("f_hht",))
            DMA("scalar", dsl(hhbuf, it, 128, SEG), hht[:],
                ("f_hht", "hhbuf"), ("hhbuf",))
            for n in range(NT4):
                MM(psb[2 + n][0:1, 0:512], onescol[:, 0:1],
                   hht[:, n * 512:(n + 1) * 512],
                   ("f_hht", "c_onescol"), f"psb{2 + n}",
                   start=(it == 0), stop=(it == NIT - 1))
        for n in range(NT4):
            VCP(shr[0:1, n * 512:(n + 1) * 512], psb[2 + n][0:1, 0:512],
                (f"psb{2 + n}",), ("f_shr",))
        ACP(shh[:], shr[:], ("f_shr",), ("f_shh",))
        TT(shr[:], shr[:], shh[:], ALU.subtract, ("f_shr", "f_shh"),
           ("f_shr",))
        ACP(shl[:], shr[:], ("f_shr",), ("f_shl",))
        DMA("sync", shbuf[0:1, :], shh[:], ("f_shh", "shbuf"), ("shbuf",))
        DMA("sync", shbuf[1:2, :], shl[:], ("f_shl", "shbuf"), ("shbuf",))
        ar4.free()

        pr.barrier()

        ar5 = bass.AutoArena(nc)
        hh2 = ar5.sbuf_tensor("b_hh2", [128, NIT * 1024], BF16)
        dwb = {i: ar5.sbuf_tensor(f"b_dwb{i}", [128, NIT * 128], BF16)
               for i in range(2)}
        x2s = {i: ar5.sbuf_tensor(f"b_x2s{i}", [128, 1024], F32)
               for i in range(2)}
        yt = {i: ar5.sbuf_tensor(f"b_yt{i}", [128, 512], F32)
              for i in range(2)}
        dsc = ar5.sbuf_tensor("b_dsc", [128, 16], F32)
        shh2 = ar5.sbuf_tensor("b_shh2", [1, SEG], BF16)
        shl2 = ar5.sbuf_tensor("b_shl2", [1, SEG], BF16)
        ar5.allocate()

        DMA("sync", dsc[:], dr["dsP"], ("b_dsc",), ("b_dsc",))
        DMA("sync", shh2[:], shbuf[0:1, :], ("shbuf", "b_shh2"),
            ("b_shh2",))
        DMA("sync", shl2[:], shbuf[1:2, :], ("shbuf", "b_shl2"),
            ("b_shl2",))
        for th in range(2):
            for ib in range(8):
                DMA("sync", hh2[:, ib * 8192:(ib + 1) * 8192],
                    bass.AP(hhbuf.tensor, ib * 8 * 128 * SEG + th * 1024,
                            [[SEG, 128], [128 * SEG, 8], [1, 1024]]),
                    ("hhbuf", "b_hh2"), ("b_hh2",))
            for dt_ in range(NDT):
                dp = dt_ % 2
                dwt, dwk = dwb[dp], f"b_dwb{dp}"
                DMA("gpsimd", dwt[:],
                    bass.AP(dr["dwP"].tensor, dt_ * NIT * 128 * 128,
                            [[128, 128], [16384, NIT], [1, 128]]),
                    (dwk,), (dwk,))
                DMA("scalar", x2s[dp][:],
                    x2buf[dt_ * 128:(dt_ + 1) * 128,
                          th * 1024:(th + 1) * 1024],
                    ("x2buf", f"b_x2s{dp}"), (f"b_x2s{dp}",))
                for nn in range(2):
                    bi = nn + 2 * dp
                    bk = f"psb{bi}"
                    for it in range(NIT):
                        MM(psb[bi][0:128, 0:512],
                           dwt[:, it * 128:(it + 1) * 128],
                           hh2[:, it * 1024 + nn * 512:
                               it * 1024 + nn * 512 + 512],
                           (dwk, "b_hh2"), bk,
                           start=(it == 0), stop=False)
                    MM(psb[bi][0:128, 0:512], c128row[:],
                       shh2[0:1, th * 1024 + nn * 512:
                            th * 1024 + nn * 512 + 512],
                       ("c_c128row", "b_shh2"), bk, start=False,
                       stop=False)
                    MM(psb[bi][0:128, 0:512], c128row[:],
                       shl2[0:1, th * 1024 + nn * 512:
                            th * 1024 + nn * 512 + 512],
                       ("c_c128row", "b_shl2"), bk, start=False, stop=True)
                    STT(yt[nn][:], psb[bi][0:128, 0:512],
                        dsc[:, dt_:dt_ + 1],
                        x2s[dp][:, nn * 512:(nn + 1) * 512],
                        ALU.mult, ALU.add,
                        (bk, "b_dsc", f"b_x2s{dp}"), (f"b_yt{nn}",))
                    DMA("sync",
                        outT[dt_ * 128:(dt_ + 1) * 128,
                             th * 1024 + nn * 512:
                             th * 1024 + nn * 512 + 512],
                        yt[nn][:], (f"b_yt{nn}", "outT"), ("outT",))
        ar5.free()

        E("gpsimd", lambda gp: None, reads=("outT",))

        pr.finalize(nc, gctx)
        nc.compile()
    return nc


CONSTS = host_consts()
_NC = None


def _get_nc():
    global _NC
    if _NC is None:
        _NC = build_core_program()
    return _NC


def kernel(**inputs):
    bf = ml_dtypes.bfloat16
    f32 = np.float32
    x = np.asarray(inputs["L0_attn_hidden_states"], f32)
    Wq = np.asarray(inputs["L0_attn_Wq"], f32)
    Wk = np.asarray(inputs["L0_attn_Wk"], f32)
    Wv = np.asarray(inputs["L0_attn_Wv"], f32)
    conv_q = np.asarray(inputs["L0_attn_conv_q"], f32)
    conv_k = np.asarray(inputs["L0_attn_conv_k"], f32)
    conv_v = np.asarray(inputs["L0_attn_conv_v"], f32)
    Wf1 = np.asarray(inputs["L0_attn_Wf1"], f32)
    Wf2 = np.asarray(inputs["L0_attn_Wf2"], f32)
    Wb = np.asarray(inputs["L0_attn_Wb"], f32)
    A_log = np.asarray(inputs["L0_attn_A_log"], f32)
    dt_bias = np.asarray(inputs["L0_attn_dt_bias"], f32)
    Wg = np.asarray(inputs["L0_attn_Wg"], f32)
    bg = np.asarray(inputs["L0_attn_bg"], f32)
    o_norm_w = np.asarray(inputs["L0_attn_o_norm_w"], f32)
    Wo = np.asarray(inputs["L0_attn_Wo"], f32)
    gate_w = np.asarray(inputs["L0_ffn_gate_w"])
    gate_scale = np.asarray(inputs["L0_ffn_gate_scale"], f32)
    up_w = np.asarray(inputs["L0_ffn_up_w"])
    up_scale = np.asarray(inputs["L0_ffn_up_scale"], f32)
    down_w = np.asarray(inputs["L0_ffn_down_w"])
    down_scale = np.asarray(inputs["L0_ffn_down_scale"], f32)

    def cc(a):
        return np.ascontiguousarray(a)

    shared = dict(
        WqP=cc(Wq.reshape(D, H, 128).transpose(1, 0, 2)),
        WkP=cc(Wk.reshape(D, H, 128).transpose(1, 0, 2)),
        WvP=cc(Wv.reshape(D, H, 128).transpose(1, 0, 2)),
        WgP=cc(Wg.reshape(D, H, 128).transpose(1, 0, 2)),
        Wf1=cc(Wf1),
        Wf2P=cc(Wf2.reshape(128, H, 128).transpose(1, 0, 2)),
        WbP=cc(Wb),
        WoP=cc(Wo.reshape(H, 128, D)),
        convP=cc(np.concatenate([conv_q.reshape(H, 128, 4),
                                 conv_k.reshape(H, 128, 4),
                                 conv_v.reshape(H, 128, 4)], axis=2)),
        dtbP=cc(dt_bias.reshape(H, 128).T),
        AlogP=cc(A_log.reshape(16, 1)),
        onwP=cc(o_norm_w.reshape(1, 128)),
        bgP=cc(bg.reshape(H, 128)),
        gwP=cc(gate_w.T.reshape(D, NIT, 128).transpose(1, 0, 2)
               .astype(bf)),
        uwP=cc(up_w.T.reshape(D, NIT, 128).transpose(1, 0, 2).astype(bf)),
        dwP=cc(down_w.reshape(16, 128, NIT, 128).transpose(0, 2, 3, 1)
               .astype(bf)),
        gsP=cc(gate_scale.reshape(NIT, 128)),
        usP=cc(up_scale.reshape(NIT, 128)),
        dsP=cc(down_scale.reshape(16, 128).T),
        **CONSTS)

    in_maps = []
    for c in range(8):
        b_, s_ = c // 4, c % 4
        xseg = cc(x[b_, s_ * SEG:(s_ + 1) * SEG, :].T)
        if s_ > 0:
            xh = cc(x[b_, s_ * SEG - 3:s_ * SEG, :].T)
        else:
            xh = np.zeros((D, 3), f32)
        gm = np.zeros((128, 8), f32)
        for j in range(8):
            if j // 4 == b_ and j % 4 < s_:
                gm[:, j] = 1.0
        in_maps.append(dict(xT=xseg, xhT=xh, gam=gm, **shared))

    nc = _get_nc()
    res = run_bass_kernel_spmd(nc, in_maps, core_ids=list(range(8)))
    out = np.empty((B, T, D), f32)
    for c in range(8):
        b_, s_ = c // 4, c % 4
        out[b_, s_ * SEG:(s_ + 1) * SEG, :] = \
            np.asarray(res.results[c]["outT"], f32).T
    return out
